# version 05
# baseline (speedup 1.0000x reference)
import numpy as np
import concourse.bass as bass
import concourse.mybir as mybir
from concourse.bass_utils import run_bass_kernel_spmd

F32 = mybir.dt.float32
BF16 = mybir.dt.bfloat16
FP8 = mybir.dt.float8e4
I32 = mybir.dt.int32
Alu = mybir.AluOpType
Act = mybir.ActivationFunctionType
AX = mybir.AxisListType

T = 8192
D = 1024
HS = 64
INNER = 4096
NG = 16
GTOK = 512
ENGINES = ("sync", "gpsimd", "vector", "scalar", "tensor")


class Ev:
    __slots__ = ("eng", "amount", "val")

    def __init__(self, eng, amount=1):
        self.eng = eng
        self.amount = amount
        self.val = None


class Prog:
    def __init__(self):
        self.ops = {e: [] for e in ENGINES}

    def add(self, eng, fn, waits=(), inc=None):
        ws = tuple(w for w in waits if w is not None)
        if inc is not None:
            assert inc.eng == eng
        self.ops[eng].append((fn, ws, inc))
        return inc

    def assign(self):
        for e in ENGINES:
            c = 0
            for fn, waits, inc in self.ops[e]:
                if inc is not None:
                    c += inc.amount
                    inc.val = c

    def check(self):
        cnt = {e: 0 for e in ENGINES}
        idx = {e: 0 for e in ENGINES}
        total = sum(len(v) for v in self.ops.values())
        done = 0
        while done < total:
            progressed = False
            for e in ENGINES:
                while idx[e] < len(self.ops[e]):
                    fn, waits, inc = self.ops[e][idx[e]]
                    ok = True
                    for w in waits:
                        assert w.val is not None, f"unassigned event wait in {e}"
                        if cnt[w.eng] < w.val:
                            ok = False
                            break
                    if not ok:
                        break
                    if inc is not None:
                        cnt[e] = inc.val
                    idx[e] += 1
                    done += 1
                    progressed = True
            if not progressed:
                detail = []
                for e in ENGINES:
                    if idx[e] < len(self.ops[e]):
                        fn, waits, inc = self.ops[e][idx[e]]
                        bad = [(w.eng, w.val, cnt[w.eng]) for w in waits
                               if cnt[w.eng] < w.val]
                        detail.append(f"{e}@{idx[e]} blocked on {bad}")
                raise RuntimeError("DEADLOCK in schedule: " + "; ".join(detail))


def build_program():
    nc = bass.Bass("TRN2", target_bir_lowering=False)
    x_d = nc.dram_tensor("x", [T, D], F32, kind="ExternalInput").ap()
    qkvw_d = nc.dram_tensor("qkv_w", [3 * HS, D], F32, kind="ExternalInput").ap()
    qkvb_d = nc.dram_tensor("qkv_b", [3 * HS], F32, kind="ExternalInput").ap()
    projw_d = nc.dram_tensor("proj_w", [D, D], F32, kind="ExternalInput").ap()
    projb_d = nc.dram_tensor("proj_b", [D], F32, kind="ExternalInput").ap()
    wg_d = nc.dram_tensor("w_gate", [INNER, D], F32, kind="ExternalInput").ap()
    wu_d = nc.dram_tensor("w_up", [INNER, D], F32, kind="ExternalInput").ap()
    wd_d = nc.dram_tensor("w_down", [D, INNER], F32, kind="ExternalInput").ap()
    y_d = nc.dram_tensor("y", [T, D], F32, kind="ExternalOutput").ap()

    from contextlib import ExitStack
    p = Prog()
    A = p.add

    def E(eng, amount=1):
        return Ev(eng, amount)

    with ExitStack() as ctx:
        def sb(name, shape, dt=F32):
            return ctx.enter_context(nc.sbuf_tensor(name, shape, dt))

        def psum(name, shape, dt=F32):
            return ctx.enter_context(nc.psum_tensor(name, shape, dt))

        b0 = psum("b0", [128, 512])
        b1 = psum("b1", [128, 512])
        b2 = psum("b2", [128, 512])
        b3 = psum("b3", [128, 512])
        b4 = psum("b4", [128, 512])
        b5 = psum("b5", [128, 512])
        b6 = psum("b6", [128, 1024], BF16)
        b7 = psum("b7", [128, 512])

        WgT8 = sb("WgT8", [128, 8 * INNER], FP8)
        WuT8 = sb("WuT8", [128, 8 * INNER], FP8)
        WdT8 = sb("WdT8", [128, 32 * D], FP8)
        qkvwT = sb("qkvwT", [128, 8 * 192])
        projT = sb("projT", [65, D])
        qb_qk = sb("qb_qk", [128, 1])
        qb_vrow = sb("qb_vrow", [1, HS])
        mask_keep = sb("mask_keep", [128, 128])
        ident_f = sb("ident_f", [128, 128])
        ident_b = sb("ident_b", [128, 128], BF16)
        ones_row = sb("ones_row", [1, 128])
        onesP = sb("onesP", [128, 128])
        S_sb = sb("S_sb", [64, 64])
        c_negD = sb("c_negD", [128, 1]); c_recD = sb("c_recD", [128, 1])
        c_eps = sb("c_eps", [128, 1]); c_127 = sb("c_127", [128, 1])
        c_half = sb("c_half", [128, 1]); c_nhalf = sb("c_nhalf", [128, 1])
        c_1_127 = sb("c_1_127", [128, 1]); c_wrec = sb("c_wrec", [128, 1])
        wgm127 = sb("wgm127", [128, 1]); wum127 = sb("wum127", [128, 1])
        wdm127 = sb("wdm127", [128, 1])
        wsc_g = sb("wsc_g", [128, 1]); wsc_u = sb("wsc_u", [128, 1])
        wsc_d = sb("wsc_d", [128, 1])

        x_sb = [sb(f"x_sb{i}", [128, 4 * D]) for i in range(2)]
        xn_s = sb("xn_s", [128, D])
        xnT_s = sb("xnT_s", [128, 8 * 128])
        XqT_s = sb("XqT_s", [128, 8 * GTOK], BF16)
        q_s = sb("q_s", [64, GTOK])
        kT_s = sb("kT_s", [64, GTOK])
        k_s = sb("k_s", [128, 4 * HS])
        v_s = sb("v_s", [128, 4 * HS])
        wei_sb = [sb(f"wei{i}", [128, GTOK]) for i in range(2)]
        q_sb = [q_s, q_s]; kT_sb = [kT_s, kT_s]
        k_sb = [k_s, k_s]; v_sb = [v_s, v_s]
        outhT_s = sb("outhT_s", [65, GTOK])
        H_s = sb("H_s", [128, INNER])
        Hq_s = sb("Hq_s", [128, INNER], BF16)
        HqT_s = sb("HqT_s", [128, 32 * 128], BF16)
        g_act_s = sb("g_act_s", [128, 512])
        g_act = [g_act_s, g_act_s]
        qs_y = sb("qs_y", [128, 512]); qs_ci = sb("qs_ci", [128, 512], I32)
        qs_f1 = wei_sb[0][:, 0:512]; qs_f2 = wei_sb[1][:, 0:512]
        xn = [xn_s, xn_s]
        xnT = [xnT_s, xnT_s]
        Xq_st = [Hq_s[:, 0:D], Hq_s[:, 0:D]]
        XqT_sb = [XqT_s, XqT_s]
        outhT = [outhT_s, outhT_s]
        H = [H_s, H_s]
        Hq = [Hq_s, Hq_s]
        HqT_sb = [HqT_s, HqT_s]

        def rs1(name):
            t = sb(name, [128, 1])
            return [t, t]

        def rs2(name):
            return [sb(f"{name}{i}", [128, 1]) for i in range(2)]
        t_sum = rs1("t_sum"); t_negm = rs1("t_negm"); t_ssq = rs1("t_ssq")
        t_var = rs1("t_var"); t_sd = rs1("t_sd"); t_rstd = rs1("t_rstd")
        t_nmr = rs1("t_nmr"); t_amax = rs1("t_amax"); t_rec = rs1("t_rec")
        t_r127 = rs1("t_r127"); t_sg = rs2("t_sg"); t_su = rs2("t_su")
        t_amxh = rs1("t_amxh"); t_rech = rs1("t_rech")
        t_r127h = rs1("t_r127h"); t_dqh = rs2("t_dqh")

        wld = [H_s, x_sb[0]]
        wabs = x_sb[1]
        wsgn_ofs = 2048
        wqbf = XqT_s
        wpart = qs_y[:, 64:96]
        t_tot = qs_y[:, 96:97]
        pe_chunk = qs_y[:, 0:64]

        sems = {e: ctx.enter_context(nc.semaphore(f"sem_{e}"))
                for e in ENGINES}
        block = ctx.enter_context(nc.Block())

        ring_f = {"evac": None}
        ring_b = {"evac": None}

        def ccopy(eng, out, in_):
            if hasattr(eng, "tensor_copy"):
                return eng.tensor_copy(out, in_)
            return eng.copy(out, in_)

        def dma(dst, src):
            def fn(eng):
                return eng.dma_start(out=dst, in_=src)
            return fn

        GPC = E("gpsimd")

        def gp_consts(g):
            g.memset(c_negD[:], -1.0 / D)
            g.memset(c_recD[:], 1.0 / D)
            g.memset(c_eps[:], 1e-5)
            g.memset(c_127[:], 127.0)
            g.memset(c_half[:], 0.5)
            g.memset(c_nhalf[:], -0.5)
            g.memset(c_1_127[:], 1.0 / 127.0)
            g.memset(c_wrec[:], 1.0 / (INNER * D))
            g.memset(ones_row[:], 1.0)
            g.memset(onesP[:], 1.0)
            g.memset(S_sb[:], 0.0)
            g.memset(outhT_s[64:65, :], 1.0)
            g.memset(mask_keep[:], 0.0)
            g.affine_select(out=mask_keep[:], in_=mask_keep[:],
                            compare_op=Alu.is_gt, fill=1.0, base=0,
                            pattern=[[-1, 128]], channel_multiplier=1)
            g.memset(ident_f[:], 0.0)
            g.affine_select(out=ident_f[:], in_=ident_f[:],
                            compare_op=Alu.not_equal, fill=1.0, base=0,
                            pattern=[[-1, 128]], channel_multiplier=1)
            g.memset(ident_b[:], 0.0)
            return g.affine_select(out=ident_b[:], in_=ident_b[:],
                                   compare_op=Alu.not_equal, fill=1.0, base=0,
                                   pattern=[[-1, 128]], channel_multiplier=1)
        A("gpsimd", gp_consts, inc=GPC)

        LD_qw = E("sync", 16)
        A("sync", dma(wld[0][:, 0:1024], qkvw_d[0:128, :]), inc=LD_qw)
        LD_qw2 = E("sync", 16)
        A("sync", dma(wld[1][0:64, 0:1024], qkvw_d[128:192, :]), inc=LD_qw2)
        LD_qb = E("sync", 16)
        A("sync", dma(qb_qk[:], qkvb_d[0:128].rearrange("(p o) -> p o", o=1)),
          inc=LD_qb)
        LD_qbv = E("sync", 16)
        A("sync", dma(qb_vrow[:], qkvb_d[128:192].rearrange("(o d) -> o d", o=1)),
          inc=LD_qbv)
        LD_pb = E("sync", 16)
        A("sync", dma(projT[64:65, :], projb_d.rearrange("(o d) -> o d", o=1)),
          inc=LD_pb)

        def f32_transposes(pieces, extra_first_wait=(), evac_eng="vector",
                           merged_out=None):
            last = None
            i = 0
            bi = 0
            while i < len(pieces):
                batch = pieces[i:i + 4]
                pe_ev = E("tensor")
                for sidx, (in_ap, out_ap, np_) in enumerate(batch):
                    def fn(eng, in_ap=in_ap, sidx=sidx, np_=np_):
                        dst = b7[0:in_ap.shape[1],
                                 sidx * 128: sidx * 128 + in_ap.shape[0]]
                        return eng.transpose(dst, in_ap,
                                             ident_f[0:np_, 0:np_])
                    w = []
                    if sidx == 0:
                        w = [ring_f["evac"], GPC] + list(extra_first_wait)
                    A("tensor", fn, waits=w,
                      inc=(pe_ev if sidx == len(batch) - 1 else None))
                ve = E(evac_eng)
                if merged_out is not None:
                    mo = merged_out[bi]
                    wdt = batch[0][0].shape[0]

                    def fn(eng, mo=mo, nb=len(batch), wdt=wdt,
                           np2=batch[0][0].shape[1]):
                        src = b7[0:np2, 0:nb * 128].rearrange(
                            "p (s i) -> p s i", i=128)[:, :, 0:wdt]
                        return ccopy(eng, mo, src)
                    A(evac_eng, fn, waits=[pe_ev], inc=ve)
                else:
                    for sidx, (in_ap, out_ap, np_) in enumerate(batch):
                        def fn(eng, in_ap=in_ap, out_ap=out_ap, sidx=sidx):
                            src = b7[0:in_ap.shape[1],
                                     sidx * 128: sidx * 128 + in_ap.shape[0]]
                            return ccopy(eng, out_ap, src)
                        A(evac_eng, fn, waits=([pe_ev] if sidx == 0 else ()),
                          inc=(ve if sidx == len(batch) - 1 else None))
                ring_f["evac"] = ve
                last = ve
                i += 4
                bi += 1
            return last

        A("tensor", None, waits=[LD_qw, LD_qw2])
        pieces = [(wld[0][:, k * 128:(k + 1) * 128],
                   qkvwT[:, k * 192: k * 192 + 128], 128) for k in range(8)]
        pieces += [(wld[1][0:64, k * 128:(k + 1) * 128],
                    qkvwT[:, k * 192 + 128: k * 192 + 192], 64)
                   for k in range(8)]
        EV_qkvwT = f32_transposes(pieces)

        prev_tr = [None, EV_qkvwT]
        for c in range(8):
            ld = E("sync", 16)
            w = [prev_tr[c % 2]] if prev_tr[c % 2] is not None else []
            if c < 2:
                w.append(EV_qkvwT)
            A("sync", dma(wld[c % 2][:, 0:1024],
                          projw_d[c * 128:(c + 1) * 128, :]),
              waits=w, inc=ld)
            rv = E("vector")

            def fn(eng, c=c):
                src = wld[c % 2][:, 0:1024].rearrange("p (r d) -> p d r", d=64)
                dst = pe_chunk[:].rearrange("p (d o) -> p d o", o=1)
                return eng.reduce_sum(dst, src, axis=AX.X)
            A("vector", fn, waits=[ld], inc=rv)
            ev = f32_transposes(
                [(pe_chunk[:, 0:64], projT[0:64, c * 128:(c + 1) * 128], 128)],
                extra_first_wait=[rv])
            prev_tr[c % 2] = ev
        EV_projT = ev

        def weight_prologue(w_dram, nchunks, ccols, dstT, dpc, wsc, wm127,
                            prev_done):
            njt = ccols // 128
            rvs = []
            for c in range(nchunks):
                ld = E("sync", 16)
                w = []
                if c == 0 and prev_done is not None:
                    w.append(prev_done)
                if c >= 2:
                    w.append(rvs[c - 2])
                A("sync", dma(wld[c % 2][:, 0:ccols],
                              w_dram[c * 128:(c + 1) * 128, :]),
                  waits=w, inc=ld)
                rv = E("vector")

                def fn(eng, c=c, ccols=ccols):
                    return eng.reduce_sum(wpart[:, c:c + 1],
                                          wld[c % 2][:, 0:ccols],
                                          axis=AX.X, apply_absolute_value=True)
                A("vector", fn, waits=[ld, GPC], inc=rv)
                rvs.append(rv)
            rv2 = E("vector")

            def fn(eng, nchunks=nchunks):
                return eng.reduce_sum(t_tot[:], wpart[:, 0:nchunks], axis=AX.X)
            A("vector", fn, waits=[rvs[-1]], inc=rv2)
            pe = E("tensor")

            def fn(eng):
                return eng.matmul(b0[:, 0:1], onesP[:], t_tot[:],
                                  start=True, stop=True)
            A("tensor", fn, waits=[rv2], inc=pe)
            vsc = E("vector")

            def fn(eng, wsc=wsc, wm127=wm127):
                eng.tensor_copy(t_tot[:], b0[:, 0:1])
                eng.tensor_single_scalar(t_tot[:], t_tot[:], c_wrec[:], Alu.mult)
                eng.tensor_single_scalar(t_tot[:], t_tot[:], c_eps[:], Alu.max)
                eng.tensor_single_scalar(wm127[:], t_tot[:], c_1_127[:],
                                         Alu.mult)
                return eng.reciprocal(wsc[:], t_tot[:])
            A("vector", fn, waits=[pe], inc=vsc)
            prev_sa = [None, None]
            prev_vq = None
            prev_pe_last = None
            ev_done = None
            hw = min(ccols, 2048)
            nhalf = ccols // hw
            for c in range(nchunks):
                ld = E("sync", 16)
                w = [vsc] if c == 0 else []
                if prev_sa[c % 2] is not None:
                    w.append(prev_sa[c % 2])
                A("sync", dma(wld[c % 2][:, 0:ccols],
                              w_dram[c * 128:(c + 1) * 128, :]),
                  waits=w, inc=ld)
                for h in range(nhalf):
                    sa = E("scalar")

                    def fn(eng, c=c, h=h, hw=hw, wsc=wsc):
                        eng.activation(out=wabs[:, 0:hw],
                                       in_=wld[c % 2][:, h * hw:(h + 1) * hw],
                                       func=Act.Abs, bias=0.0,
                                       scale=wsc[:, 0:1])
                        return eng.activation(
                            out=wabs[:, wsgn_ofs:wsgn_ofs + hw],
                            in_=wld[c % 2][:, h * hw:(h + 1) * hw],
                            func=Act.Sign, bias=0.0, scale=1.0)
                    ws_ = [ld, vsc]
                    if prev_vq is not None:
                        ws_.append(prev_vq)
                    A("scalar", fn, waits=ws_, inc=sa)
                    if h == nhalf - 1:
                        prev_sa[c % 2] = sa
                    vq = E("vector")

                    def fn(eng, hw=hw):
                        eng.tensor_single_scalar(wabs[:, 0:hw], wabs[:, 0:hw],
                                                 c_half[:], Alu.is_ge)
                        return eng.tensor_tensor(
                            wqbf[:, 0:hw], wabs[:, 0:hw],
                            wabs[:, wsgn_ofs:wsgn_ofs + hw], Alu.mult)
                    wv = [sa]
                    if prev_pe_last is not None:
                        wv.append(prev_pe_last)
                    A("vector", fn, waits=wv, inc=vq)
                    prev_vq = vq
                    njt_h = hw // 128
                    nbatch = max(1, njt_h // 8)
                    for bb in range(nbatch):
                        jlo = bb * 8
                        jn = min(8, njt_h - jlo)
                        pe_ev = E("tensor")
                        for sidx in range(jn):
                            jj = jlo + sidx

                            def fn(eng, jj=jj, sidx=sidx):
                                return eng.transpose(
                                    b6[:, sidx * 128:(sidx + 1) * 128],
                                    wqbf[:, jj * 128:(jj + 1) * 128],
                                    ident_b[:])
                            w = []
                            if sidx == 0:
                                w = [vq]
                                if ring_b["evac"] is not None:
                                    w.append(ring_b["evac"])
                            A("tensor", fn, waits=w,
                              inc=(pe_ev if sidx == jn - 1 else None))
                        prev_pe_last = pe_ev
                        ve = E("vector")
                        for sidx in range(jn):
                            j = h * njt_h + jlo + sidx

                            def fn(eng, j=j, sidx=sidx, c=c, dstT=dstT,
                                   dpc=dpc):
                                return eng.tensor_copy(
                                    dstT[:, j * dpc + c * 128:
                                         j * dpc + (c + 1) * 128],
                                    b6[:, sidx * 128:(sidx + 1) * 128])
                            A("vector", fn,
                              waits=([pe_ev] if sidx == 0 else ()),
                              inc=(ve if sidx == jn - 1 else None))
                        ring_b["evac"] = ve
                        ev_done = ve
            return ev_done

        EV_wg = weight_prologue(wg_d, 32, D, WgT8, INNER, wsc_g, wgm127,
                                EV_projT)
        EV_wu = weight_prologue(wu_d, 32, D, WuT8, INNER, wsc_u, wum127, EV_wg)
        EV_wd = weight_prologue(wd_d, 8, INNER, WdT8, D, wsc_d, wdm127, EV_wu)
        EV_codes = EV_wd

        XL = [None] * NG
        EO = [None] * NG
        EV_fin = [None] * (4 * NG)
        EP_state_g = [None] * NG
        prev_xnT_evac = [None]
        prev_vtok_pe = [None]
        prev_gu_last = [None]
        prev_down_pe = [None]
        prev_EVH_last2 = [None, None]
        EV_S_prev = None

        for g in range(NG):
            par = g % 2
            XL[g] = E("sync", 16)
            w = [EO[g - 2]] if g >= 2 else [EV_codes]
            src = x_d[g * GTOK:(g + 1) * GTOK, :].rearrange(
                "(c p) d -> p c d", p=128)
            dst = x_sb[par][:].rearrange("p (c d) -> p c d", d=D)
            A("sync", dma(dst, src), waits=w, inc=XL[g])

            EP_qkc = [None] * 4
            EV_xqTe = [None] * 4
            EV_vtok = [None] * 4
            vtok_pe = [None] * 4
            xnT_evac = [None] * 4
            for c in range(4):
                k = 4 * g + c
                kp = k % 2
                xg = x_sb[par][:, c * D:(c + 1) * D]

                EV_negm = E("vector")

                def fn(eng, xg=xg, kp=kp):
                    eng.reduce_sum(t_sum[kp][:], xg, axis=AX.X)
                    return eng.tensor_single_scalar(
                        t_negm[kp][:], t_sum[kp][:], c_negD[:], Alu.mult)
                A("vector", fn, waits=[XL[g], GPC], inc=EV_negm)

                ES_ssq = E("scalar")

                def fn(eng, xg=xg, kp=kp):
                    return eng.activation(out=xn[kp][:], in_=xg,
                                          func=Act.Square,
                                          bias=t_negm[kp][:], scale=1.0,
                                          accum_out=t_ssq[kp][:])
                A("scalar", fn, waits=[EV_negm, prev_xnT_evac[0]], inc=ES_ssq)

                EV_var = E("vector")

                def fn(eng, kp=kp):
                    eng.tensor_single_scalar(t_var[kp][:], t_ssq[kp][:],
                                             c_recD[:], Alu.mult)
                    return eng.tensor_single_scalar(
                        t_var[kp][:], t_var[kp][:], c_eps[:], Alu.add)
                A("vector", fn, waits=[ES_ssq], inc=EV_var)

                ES_sd = E("scalar")

                def fn(eng, kp=kp):
                    return eng.activation(out=t_sd[kp][:], in_=t_var[kp][:],
                                          func=Act.Sqrt, bias=0.0, scale=1.0)
                A("scalar", fn, waits=[EV_var], inc=ES_sd)

                EV_rf = E("vector")

                def fn(eng, kp=kp):
                    return eng.reciprocal(t_rstd[kp][:], t_sd[kp][:])
                A("vector", fn, waits=[ES_sd], inc=EV_rf)
                EV_nmr = E("vector")

                def fn(eng, kp=kp):
                    return eng.tensor_single_scalar(
                        t_nmr[kp][:], t_negm[kp][:], t_rstd[kp][:], Alu.mult)
                A("vector", fn, waits=[EV_rf], inc=EV_nmr)

                ES_xn = E("scalar")

                def fn(eng, xg=xg, kp=kp):
                    return eng.activation(out=xn[kp][:], in_=xg,
                                          func=Act.Identity,
                                          bias=t_nmr[kp][:],
                                          scale=t_rstd[kp][:])
                A("scalar", fn, waits=[EV_nmr], inc=ES_xn)

                EV_r127f = E("vector")

                def fn(eng, kp=kp):
                    eng.reduce_max(t_amax[kp][:], xn[kp][:], axis=AX.X,
                                   apply_absolute_value=True)
                    eng.tensor_single_scalar(t_amax[kp][:], t_amax[kp][:],
                                             c_eps[:], Alu.max)
                    eng.reciprocal(t_rec[kp][:], t_amax[kp][:])
                    return eng.tensor_single_scalar(
                        t_r127[kp][:], t_rec[kp][:], c_127[:], Alu.mult)
                A("vector", fn, waits=[ES_xn], inc=EV_r127f)

                EV_gu = E("vector")

                def fn(eng, kp=kp):
                    eng.tensor_single_scalar(t_sg[kp][:], t_amax[kp][:],
                                             wgm127[:], Alu.mult)
                    return eng.tensor_single_scalar(
                        t_su[kp][:], t_amax[kp][:], wum127[:], Alu.mult)
                A("vector", fn, waits=[EV_r127f, EV_codes], inc=EV_gu)

                EV_xq = E("vector")
                for sl in range(2):
                    def fn(eng, sl=sl, kp=kp):
                        src = xn[kp][:, sl * 512:(sl + 1) * 512]
                        eng.tensor_single_scalar(qs_y[:], src,
                                                 t_r127[kp][:], Alu.mult)
                        eng.tensor_copy(qs_ci[:], qs_y[:])
                        eng.scalar_tensor_tensor(qs_f1[:], qs_ci[:], c_half[:],
                                                 qs_y[:], Alu.add, Alu.is_le)
                        eng.scalar_tensor_tensor(qs_f2[:], qs_ci[:],
                                                 c_nhalf[:], qs_y[:],
                                                 Alu.add, Alu.is_ge)
                        eng.tensor_sub(qs_f1[:], qs_f1[:], qs_f2[:])
                        return eng.tensor_tensor(
                            Xq_st[kp][:, sl * 512:(sl + 1) * 512],
                            qs_f1[:], qs_ci[:], Alu.add)
                    A("vector", fn, waits=([EV_gu] if sl == 0 else ()),
                      inc=(EV_xq if sl == 1 else None))

                A("tensor", None, waits=[ES_xn])
                xnT3 = xnT[kp][:].rearrange("p (s i) -> p s i", i=128)
                ev = f32_transposes(
                    [(xn[kp][:, j * 128:(j + 1) * 128],
                      xnT[kp][:, j * 128:(j + 1) * 128], 128)
                     for j in range(8)],
                    extra_first_wait=([prev_vtok_pe[0]]
                                      if prev_vtok_pe[0] is not None else []),
                    merged_out=[xnT3[:, 0:4, :], xnT3[:, 4:8, :]])
                xnT_evac[c] = ev

                pe_ev = E("tensor")
                for s in range(8):
                    def fn(eng, s=s, kp=kp):
                        return eng.transpose(
                            b6[:, s * 128:(s + 1) * 128],
                            Xq_st[kp][:, s * 128:(s + 1) * 128], ident_b[:])
                    w = []
                    if s == 0:
                        w = [EV_xq]
                        if ring_b["evac"] is not None:
                            w.append(ring_b["evac"])
                    A("tensor", fn, waits=w, inc=(pe_ev if s == 7 else None))
                ve = E("scalar")

                def fn(eng, par=par, c=c):
                    dst = XqT_sb[par][:].rearrange(
                        "p (s t) -> p s t", t=GTOK)[:, :, c * 128:(c + 1) * 128]
                    src = b6[:].rearrange("p (s i) -> p s i", i=128)
                    return ccopy(eng, dst, src)
                w = [pe_ev]
                if c == 0 and prev_gu_last[0] is not None:
                    w.append(prev_gu_last[0])
                A("scalar", fn, waits=w, inc=ve)
                ring_b["evac"] = ve
                EV_xqTe[c] = ve

                pe_ev = E("tensor")
                for kk in range(8):
                    def fn(eng, kk=kk, c=c, kp=kp):
                        return eng.matmul(
                            b0[:, c * 128:(c + 1) * 128],
                            qkvwT[:, kk * 192: kk * 192 + 128],
                            xnT[kp][:, kk * 128:(kk + 1) * 128],
                            start=(kk == 0), stop=(kk == 7))
                    w = []
                    if kk == 0:
                        w = [xnT_evac[c], EV_qkvwT]
                        if c == 0:
                            w.append(EV_fin[4 * g - 1] if g > 0 else EV_codes)
                    A("tensor", fn, waits=w, inc=(pe_ev if kk == 7 else None))
                EP_qkc[c] = pe_ev

                pe_ev = E("tensor")
                for kk in range(9):
                    def fn(eng, kk=kk, kp=kp):
                        dst = b7[:, 0:64]
                        if kk < 8:
                            return eng.matmul(
                                dst, xnT[kp][:, kk * 128:(kk + 1) * 128],
                                qkvwT[:, kk * 192 + 128: kk * 192 + 192],
                                start=(kk == 0), stop=False)
                        return eng.matmul(dst, ones_row[:], qb_vrow[:],
                                          start=False, stop=True)
                    w = []
                    if kk == 0:
                        w = [ring_f["evac"], LD_qbv]
                    A("tensor", fn, waits=w, inc=(pe_ev if kk == 8 else None))
                vtok_pe[c] = pe_ev
                prev_vtok_pe[0] = pe_ev
                ve = E("scalar")

                def fn(eng, par=par, c=c):
                    return ccopy(eng, v_sb[par][:, c * HS:(c + 1) * HS],
                                 b7[:, 0:64])
                w = [pe_ev]
                if c == 0 and g >= 1:
                    w.append(EP_state_g[g - 1])
                A("scalar", fn, waits=w, inc=ve)
                ring_f["evac"] = ve
                EV_vtok[c] = ve
                prev_xnT_evac[0] = xnT_evac[c]

            EV_qk = E("scalar")

            def fn(eng, par=par):
                eng.activation(out=q_sb[par][:], in_=b0[0:64, :],
                               func=Act.Identity, bias=qb_qk[0:64, 0:1],
                               scale=1.0)
                return eng.activation(out=kT_sb[par][:], in_=b0[64:128, :],
                                      func=Act.Identity,
                                      bias=qb_qk[64:128, 0:1], scale=1.0)
            w = [EP_qkc[3], LD_qb]
            if g >= 1:
                w.append(EP_state_g[g - 1])
            A("scalar", fn, waits=w, inc=EV_qk)

            A("tensor", None, waits=[EV_qk])
            k3 = k_sb[par][:].rearrange("p (s i) -> p s i", i=HS)
            EV_ktok = f32_transposes(
                [(kT_sb[par][0:64, c * 128:(c + 1) * 128],
                  k_sb[par][:, c * HS:(c + 1) * HS], 64) for c in range(4)],
                extra_first_wait=([EP_state_g[g - 1]] if g >= 1 else []),
                merged_out=[k3[:, 0:4, :]])

            EP_inter = E("tensor")

            def fn(eng, par=par):
                return eng.matmul(b4[0:64, 0:GTOK], S_sb[:], q_sb[par][:],
                                  start=True, stop=False)
            w = [EV_qk]
            if EV_S_prev is not None:
                w.append(EV_S_prev)
            if g > 0:
                w.append(EV_fin[4 * g - 1])
            A("tensor", fn, waits=w, inc=EP_inter)

            EV_wei = [None] * 4
            EP_intra = [None] * 4
            for c in range(4):
                bank = b2 if c % 2 == 0 else b3
                ncols = GTOK - c * 128
                pe_ev = E("tensor")

                def fn(eng, c=c, bank=bank, ncols=ncols, par=par):
                    return eng.matmul(bank[:, 0:ncols],
                                      kT_sb[par][0:64, c * 128:(c + 1) * 128],
                                      q_sb[par][0:64, c * 128:GTOK],
                                      start=True, stop=True)
                w = []
                if c >= 2:
                    w.append(EV_wei[c - 2])
                elif g > 0:
                    w.append(EV_fin[4 * g - 1])
                A("tensor", fn, waits=w, inc=pe_ev)
                ve = E("vector")

                def fn(eng, c=c, bank=bank, ncols=ncols):
                    wsl = wei_sb[c % 2]
                    last = eng.tensor_tensor(wsl[:, 0:128], bank[:, 0:128],
                                             mask_keep[:], Alu.mult)
                    if ncols > 128:
                        last = eng.tensor_copy(wsl[:, 128:ncols],
                                               bank[:, 128:ncols])
                    return last
                w = [pe_ev]
                if c >= 2:
                    w.append(EP_intra[c - 2])
                A("vector", fn, waits=w, inc=ve)
                EV_wei[c] = ve
                pe_ev = E("tensor")

                def fn(eng, c=c, ncols=ncols, par=par):
                    return eng.matmul(b4[0:64, c * 128:GTOK],
                                      v_sb[par][:, c * HS:(c + 1) * HS],
                                      wei_sb[c % 2][:, 0:ncols],
                                      start=False, stop=(c == 3))
                A("tensor", fn, waits=[ve, EV_vtok[c]], inc=pe_ev)
                EP_intra[c] = pe_ev

            EP_state = E("tensor")
            for c in range(4):
                def fn(eng, c=c, par=par):
                    return eng.matmul(b5[0:64, 0:64],
                                      k_sb[par][:, c * HS:(c + 1) * HS],
                                      v_sb[par][:, c * HS:(c + 1) * HS],
                                      start=(c == 0), stop=(c == 3))
                w = []
                if c == 0:
                    w = [EV_ktok, EP_inter]
                    if g > 0:
                        w.append(EV_fin[4 * g - 1])
                A("tensor", fn, waits=w, inc=(EP_state if c == 3 else None))
            EP_state_g[g] = EP_state

            EV_S_upd = E("vector")

            def fn(eng):
                return eng.tensor_tensor(S_sb[:], S_sb[:], b5[0:64, 0:64],
                                         Alu.add)
            A("vector", fn, waits=[EP_state], inc=EV_S_upd)
            EV_S_prev = EV_S_upd

            EV_outh = E("scalar")

            def fn(eng, par=par):
                return ccopy(eng, outhT[par][0:64, :], b4[0:64, :])
            A("scalar", fn, waits=[EP_intra[3]], inc=EV_outh)

            EV_xp = [None] * 4
            for c in range(4):
                pe_ev = E("tensor")

                def fn(eng, c=c, par=par):
                    eng.matmul(b0[:], outhT[par][:, c * 128:(c + 1) * 128],
                               projT[:, 0:512], start=True, stop=True)
                    return eng.matmul(b1[:],
                                      outhT[par][:, c * 128:(c + 1) * 128],
                                      projT[:, 512:1024],
                                      start=True, stop=True)
                w = [EV_outh, EV_projT, LD_pb]
                if c == 0:
                    w.append(EV_qk)
                    if g > 0:
                        w.append(EV_fin[4 * g - 1])
                else:
                    w.append(EV_xp[c - 1])
                A("tensor", fn, waits=w, inc=pe_ev)
                ve = E("vector")

                def fn(eng, c=c, par=par):
                    xs = x_sb[par]
                    eng.tensor_tensor(xs[:, c * D: c * D + 512],
                                      xs[:, c * D: c * D + 512], b0[:],
                                      Alu.add)
                    return eng.tensor_tensor(xs[:, c * D + 512:(c + 1) * D],
                                             xs[:, c * D + 512:(c + 1) * D],
                                             b1[:], Alu.add)
                A("vector", fn, waits=[pe_ev], inc=ve)
                EV_xp[c] = ve

            gu_last = None
            pend = None
            prev_hqT_pe = [None]
            for c in range(4):
                k = 4 * g + c
                kp = k % 2
                EV_H = [None] * 8
                for npair in range(4):
                    pe_ev = E("tensor")
                    for kk in range(8):
                        def fn(eng, kk=kk, npair=npair, par=par, c=c):
                            lhs = XqT_sb[par][:, kk * GTOK + c * 128:
                                              kk * GTOK + (c + 1) * 128]
                            n0 = 2 * npair
                            n1 = 2 * npair + 1
                            st = (kk == 0)
                            sp = (kk == 7)
                            eng.matmul(b2[:], lhs,
                                       WgT8[:, kk * INNER + n0 * 512:
                                            kk * INNER + (n0 + 1) * 512],
                                       start=st, stop=sp)
                            eng.matmul(b3[:], lhs,
                                       WuT8[:, kk * INNER + n0 * 512:
                                            kk * INNER + (n0 + 1) * 512],
                                       start=st, stop=sp)
                            eng.matmul(b4[:], lhs,
                                       WgT8[:, kk * INNER + n1 * 512:
                                            kk * INNER + (n1 + 1) * 512],
                                       start=st, stop=sp)
                            return eng.matmul(
                                b5[:], lhs,
                                WuT8[:, kk * INNER + n1 * 512:
                                     kk * INNER + (n1 + 1) * 512],
                                start=st, stop=sp)
                        w = []
                        if kk == 0:
                            w = [EV_xqTe[c], EV_codes]
                            if npair >= 1:
                                w += [EV_H[2 * npair - 2], EV_H[2 * npair - 1]]
                            elif c == 0:
                                w += [EV_wei[3], EV_wei[2], EV_outh, EV_S_upd]
                            else:
                                w += [prev_EVH_last2[0], prev_EVH_last2[1]]
                        A("tensor", fn, waits=w,
                          inc=(pe_ev if kk == 7 else None))
                    if npair == 3 and c == 3:
                        gu_last = pe_ev
                    for half in range(2):
                        n = 2 * npair + half
                        Gb = b2 if half == 0 else b4
                        Ub = b3 if half == 0 else b5
                        sa = E("scalar")

                        def fn(eng, Gb=Gb, kp=kp):
                            return eng.activation(out=g_act_s[:], in_=Gb[:],
                                                  func=Act.Silu, bias=0.0,
                                                  scale=t_sg[kp][:, 0:1])
                        w = [pe_ev, EV_gu]
                        if n >= 1:
                            w.append(EV_H[n - 1])
                        elif c > 0:
                            w.append(prev_EVH_last2[1])
                        A("scalar", fn, waits=w, inc=sa)
                        ve = E("vector")

                        def fn(eng, n=n, Ub=Ub, kp=kp):
                            return eng.scalar_tensor_tensor(
                                H[kp][:, n * 512:(n + 1) * 512], Ub[:],
                                t_su[kp][:, 0:1], g_act_s[:],
                                Alu.mult, Alu.mult)
                        A("vector", fn, waits=[sa], inc=ve)
                        EV_H[n] = ve
                prev_EVH_last2 = [EV_H[6], EV_H[7]]

                EV_hf = E("vector")

                def fn(eng, kp=kp):
                    eng.reduce_max(t_amxh[kp][:], H[kp][:], axis=AX.X,
                                   apply_absolute_value=True)
                    eng.tensor_single_scalar(t_amxh[kp][:], t_amxh[kp][:],
                                             c_eps[:], Alu.max)
                    eng.reciprocal(t_rech[kp][:], t_amxh[kp][:])
                    return eng.tensor_single_scalar(
                        t_r127h[kp][:], t_rech[kp][:], c_127[:], Alu.mult)
                A("vector", fn, waits=[EV_H[7]], inc=EV_hf)
                EV_dqh = E("vector")

                def fn(eng, kp=kp):
                    return eng.tensor_single_scalar(
                        t_dqh[kp][:], t_amxh[kp][:], wdm127[:], Alu.mult)
                A("vector", fn, waits=[EV_hf], inc=EV_dqh)

                EV_hq = E("vector")
                for sl in range(8):
                    def fn(eng, sl=sl, kp=kp):
                        hs = H[kp][:, sl * 512:(sl + 1) * 512]
                        eng.tensor_single_scalar(qs_y[:], hs, t_r127h[kp][:],
                                                 Alu.mult)
                        eng.tensor_copy(qs_ci[:], qs_y[:])
                        eng.scalar_tensor_tensor(qs_f1[:], qs_ci[:], c_half[:],
                                                 qs_y[:], Alu.add, Alu.is_le)
                        eng.scalar_tensor_tensor(qs_f2[:], qs_ci[:],
                                                 c_nhalf[:], qs_y[:],
                                                 Alu.add, Alu.is_ge)
                        eng.tensor_sub(qs_f1[:], qs_f1[:], qs_f2[:])
                        return eng.tensor_tensor(
                            Hq[kp][:, sl * 512:(sl + 1) * 512],
                            qs_f1[:], qs_ci[:], Alu.add)
                    w = []
                    if sl == 0:
                        w = [EV_dqh]
                        if prev_hqT_pe[0] is not None:
                            w.append(prev_hqT_pe[0])
                    A("vector", fn, waits=w,
                      inc=(EV_hq if sl == 7 else None))

                def emit_down(pk, pc, hqTe_last, dqh_ev):
                    EP_down = E("tensor")
                    for j in range(32):
                        for n2 in range(2):
                            bank = b0 if n2 == 0 else b1

                            def fn(eng, j=j, bank=bank, n2=n2):
                                return eng.matmul(
                                    bank[:],
                                    HqT_s[:, j * 128:(j + 1) * 128],
                                    WdT8[:, j * D + n2 * 512:
                                         j * D + (n2 + 1) * 512],
                                    start=(j == 0), stop=(j == 31))
                            w = []
                            if j == 0 and n2 == 0:
                                w = [hqTe_last, EV_xp[3]]
                                if pk > 0 and EV_fin[pk - 1] is not None:
                                    w.append(EV_fin[pk - 1])
                            A("tensor", fn, waits=w,
                              inc=(EP_down if (j == 31 and n2 == 1)
                                   else None))
                    prev_down_pe[0] = EP_down
                    fin = E("vector")

                    def fn(eng, pc=pc, par=par, pk=pk):
                        xs = x_sb[par]
                        kpp = pk % 2
                        eng.scalar_tensor_tensor(
                            xs[:, pc * D: pc * D + 512], b0[:],
                            t_dqh[kpp][:], xs[:, pc * D: pc * D + 512],
                            Alu.mult, Alu.add)
                        return eng.scalar_tensor_tensor(
                            xs[:, pc * D + 512:(pc + 1) * D], b1[:],
                            t_dqh[kpp][:],
                            xs[:, pc * D + 512:(pc + 1) * D],
                            Alu.mult, Alu.add)
                    A("vector", fn, waits=[EP_down, dqh_ev], inc=fin)
                    EV_fin[pk] = fin

                if pend is not None:
                    emit_down(*pend)

                hqTe_last = None
                for bb in range(4):
                    pe_ev = E("tensor")
                    for sidx in range(8):
                        j = bb * 8 + sidx

                        def fn(eng, j=j, sidx=sidx, kp=kp):
                            return eng.transpose(
                                b6[:, sidx * 128:(sidx + 1) * 128],
                                Hq[kp][:, j * 128:(j + 1) * 128], ident_b[:])
                        w = []
                        if sidx == 0:
                            w = [EV_hq]
                            if ring_b["evac"] is not None:
                                w.append(ring_b["evac"])
                        A("tensor", fn, waits=w,
                          inc=(pe_ev if sidx == 7 else None))
                    if bb == 3:
                        prev_hqT_pe[0] = pe_ev
                    ve = E("scalar")

                    def fn(eng, bb=bb):
                        return ccopy(eng,
                                     HqT_s[:, bb * 1024:(bb + 1) * 1024],
                                     b6[:])
                    A("scalar", fn, waits=[pe_ev], inc=ve)
                    ring_b["evac"] = ve
                    hqTe_last = ve
                pend = (k, c, hqTe_last, EV_dqh)
            emit_down(*pend)
            prev_gu_last[0] = gu_last

            EO[g] = E("sync", 16)
            dstap = y_d[g * GTOK:(g + 1) * GTOK, :].rearrange(
                "(c p) d -> p c d", p=128)
            osrc = x_sb[par][:].rearrange("p (c d) -> p c d", d=D)
            A("sync", dma(dstap, osrc), waits=[EV_fin[4 * g + 3]],
              inc=EO[g])

        A("sync", None, waits=[EO[NG - 1]])

        p.assign()
        p.check()

        def emit(eng_obj, eng_name):
            for fn, waits, inc in p.ops[eng_name]:
                for w in waits:
                    eng_obj.wait_ge(sems[w.eng], w.val)
                instr = fn(eng_obj) if fn is not None else None
                if inc is not None:
                    instr.then_inc(sems[inc.eng], inc.amount)

        @block.sync
        def _(e):
            emit(e, "sync")

        @block.gpsimd
        def _(e):
            emit(e, "gpsimd")

        @block.vector
        def _(e):
            emit(e, "vector")

        @block.scalar
        def _(e):
            emit(e, "scalar")

        @block.tensor
        def _(e):
            emit(e, "tensor")

    return nc


_NC_CACHE = {}


def kernel(L0_attn_ids, L0_attn_qkv_w, L0_attn_qkv_b, L0_attn_proj_w,
           L0_attn_proj_b, L0_ffn_w_gate, L0_ffn_w_up, L0_ffn_w_down):
    x = np.ascontiguousarray(np.asarray(L0_attn_ids, dtype=np.float32))
    B = x.shape[0]
    if "nc" not in _NC_CACHE:
        _NC_CACHE["nc"] = build_program()
    nc = _NC_CACHE["nc"]
    common = {
        "qkv_w": np.ascontiguousarray(np.asarray(L0_attn_qkv_w, np.float32)),
        "qkv_b": np.ascontiguousarray(np.asarray(L0_attn_qkv_b, np.float32)),
        "proj_w": np.ascontiguousarray(np.asarray(L0_attn_proj_w, np.float32)),
        "proj_b": np.ascontiguousarray(np.asarray(L0_attn_proj_b, np.float32)),
        "w_gate": np.ascontiguousarray(np.asarray(L0_ffn_w_gate, np.float32)),
        "w_up": np.ascontiguousarray(np.asarray(L0_ffn_w_up, np.float32)),
        "w_down": np.ascontiguousarray(np.asarray(L0_ffn_w_down, np.float32)),
    }
    in_maps = [dict(common, x=np.ascontiguousarray(x[b])) for b in range(B)]
    res = run_bass_kernel_spmd(nc, in_maps, core_ids=list(range(B)))
    return np.stack([np.asarray(r["y"], dtype=np.float32)
                     for r in res.results], axis=0)


# version 07
# speedup vs baseline: 1.4385x; 1.4385x over previous
import numpy as np
import concourse.bass as bass
import concourse.mybir as mybir
from concourse.bass_utils import run_bass_kernel_spmd

F32 = mybir.dt.float32
BF16 = mybir.dt.bfloat16
FP8 = mybir.dt.float8e4
I32 = mybir.dt.int32
Alu = mybir.AluOpType
Act = mybir.ActivationFunctionType
AX = mybir.AxisListType

T = 8192
D = 1024
HS = 64
INNER = 4096
NG = 16
GTOK = 512
ENGINES = ("sync", "gpsimd", "vector", "scalar", "tensor")


class Ev:
    __slots__ = ("eng", "amount", "val")

    def __init__(self, eng, amount=1):
        self.eng = eng
        self.amount = amount
        self.val = None


class Prog:
    def __init__(self):
        self.ops = {e: [] for e in ENGINES}

    def add(self, eng, fn, waits=(), inc=None):
        ws = tuple(w for w in waits if w is not None)
        if inc is not None:
            assert inc.eng == eng
        self.ops[eng].append((fn, ws, inc))
        return inc

    def assign(self):
        for e in ENGINES:
            c = 0
            for fn, waits, inc in self.ops[e]:
                if inc is not None:
                    c += inc.amount
                    inc.val = c

    def check(self):
        cnt = {e: 0 for e in ENGINES}
        idx = {e: 0 for e in ENGINES}
        total = sum(len(v) for v in self.ops.values())
        done = 0
        while done < total:
            progressed = False
            for e in ENGINES:
                while idx[e] < len(self.ops[e]):
                    fn, waits, inc = self.ops[e][idx[e]]
                    ok = True
                    for w in waits:
                        assert w.val is not None, f"unassigned event wait in {e}"
                        if cnt[w.eng] < w.val:
                            ok = False
                            break
                    if not ok:
                        break
                    if inc is not None:
                        cnt[e] = inc.val
                    idx[e] += 1
                    done += 1
                    progressed = True
            if not progressed:
                detail = []
                for e in ENGINES:
                    if idx[e] < len(self.ops[e]):
                        fn, waits, inc = self.ops[e][idx[e]]
                        bad = [(w.eng, w.val, cnt[w.eng]) for w in waits
                               if cnt[w.eng] < w.val]
                        detail.append(f"{e}@{idx[e]} blocked on {bad}")
                raise RuntimeError("DEADLOCK in schedule: " + "; ".join(detail))


def build_program():
    nc = bass.Bass("TRN2", target_bir_lowering=False)
    x_d = nc.dram_tensor("x", [T, D], F32, kind="ExternalInput").ap()
    qkvw_d = nc.dram_tensor("qkv_w", [3 * HS, D], F32, kind="ExternalInput").ap()
    qkvb_d = nc.dram_tensor("qkv_b", [3 * HS], F32, kind="ExternalInput").ap()
    projw_d = nc.dram_tensor("proj_w", [D, D], F32, kind="ExternalInput").ap()
    projb_d = nc.dram_tensor("proj_b", [D], F32, kind="ExternalInput").ap()
    wg_d = nc.dram_tensor("w_gate", [INNER, D], F32, kind="ExternalInput").ap()
    wu_d = nc.dram_tensor("w_up", [INNER, D], F32, kind="ExternalInput").ap()
    wd_d = nc.dram_tensor("w_down", [D, INNER], F32, kind="ExternalInput").ap()
    y_d = nc.dram_tensor("y", [T, D], F32, kind="ExternalOutput").ap()

    from contextlib import ExitStack
    p = Prog()
    A = p.add

    def E(eng, amount=1):
        return Ev(eng, amount)

    with ExitStack() as ctx:
        def sb(name, shape, dt=F32):
            return ctx.enter_context(nc.sbuf_tensor(name, shape, dt))

        def psum(name, shape, dt=F32):
            return ctx.enter_context(nc.psum_tensor(name, shape, dt))

        b0 = psum("b0", [128, 512])
        b1 = psum("b1", [128, 512])
        b2 = psum("b2", [128, 512])
        b3 = psum("b3", [128, 512])
        b4 = psum("b4", [128, 512])
        b5 = psum("b5", [128, 512])
        b6 = psum("b6", [128, 1024], BF16)
        b7 = psum("b7", [128, 512])

        WgT8 = sb("WgT8", [128, 8 * INNER], FP8)
        WuT8 = sb("WuT8", [128, 8 * INNER], FP8)
        WdT8 = sb("WdT8", [128, 32 * D], FP8)
        qkvwT = sb("qkvwT", [128, 8 * 192])
        projT = sb("projT", [65, D])
        qb_qk = sb("qb_qk", [128, 1])
        qb_vrow = sb("qb_vrow", [1, HS])
        qb_krow = sb("qb_krow", [1, HS])
        mask_keep = sb("mask_keep", [128, 128])
        ident_f = sb("ident_f", [128, 128])
        ident_b = sb("ident_b", [128, 128], BF16)
        onesP = sb("onesP", [128, 128])
        ones_row = onesP[0:1, 0:128]
        S_sb = sb("S_sb", [64, 64])
        S_tmp = sb("S_tmp", [64, 64])
        c_negD = sb("c_negD", [128, 1]); c_recD = sb("c_recD", [128, 1])
        c_eps = sb("c_eps", [128, 1]); c_127 = sb("c_127", [128, 1])
        c_half = sb("c_half", [128, 1]); c_nhalf = sb("c_nhalf", [128, 1])
        c_1_127 = sb("c_1_127", [128, 1]); c_wrec = sb("c_wrec", [128, 1])
        wgm127 = sb("wgm127", [128, 1]); wum127 = sb("wum127", [128, 1])
        wdm127 = sb("wdm127", [128, 1])
        wsc_g = sb("wsc_g", [128, 1]); wsc_u = sb("wsc_u", [128, 1])
        wsc_d = sb("wsc_d", [128, 1])

        x_sb = [sb(f"x_sb{i}", [128, 4 * D]) for i in range(2)]
        xn_s = sb("xn_s", [128, D])
        xnT_s = sb("xnT_s", [128, 8 * 128])
        XqT_s = sb("XqT_s", [128, 8 * GTOK], BF16)
        q_s = sb("q_s", [64, GTOK])
        kT_s = sb("kT_s", [64, GTOK])
        k_s = sb("k_s", [128, 4 * HS])
        v_s = sb("v_s", [128, 4 * HS])
        wei_s = sb("wei_s", [128, GTOK])
        wei_sb = [wei_s, wei_s]
        Xq_st = sb("Xq_st", [128, D], BF16)
        q_sb = [q_s, q_s]; kT_sb = [kT_s, kT_s]
        k_sb = [k_s, k_s]; v_sb = [v_s, v_s]
        outhT_s = sb("outhT_s", [65, GTOK])
        H_s = sb("H_s", [128, INNER])
        Hq_s = sb("Hq_s", [128, INNER], BF16)
        HqT_s = sb("HqT_s", [128, 32 * 128], BF16)
        g_act_s = sb("g_act_s", [128, 512])
        g_act = [g_act_s, g_act_s]
        qs_y = sb("qs_y", [128, 512]); qs_ci = sb("qs_ci", [128, 512], I32)
        amx8 = sb("amx8", [128, 8])
        qs_f1 = wei_s[:, 0:512]
        xn = [xn_s, xn_s]
        xnT = [xnT_s, xnT_s]
        XqT_sb = [XqT_s, XqT_s]
        outhT = [outhT_s, outhT_s]
        H = [H_s, H_s]
        Hq = [Hq_s, Hq_s]
        HqT_sb = [HqT_s, HqT_s]

        def rs1(name):
            t = sb(name, [128, 1])
            return [t, t]

        def rs2(name):
            return [sb(f"{name}{i}", [128, 1]) for i in range(2)]
        t_sum = rs1("t_sum"); t_negm = rs1("t_negm"); t_ssq = rs1("t_ssq")
        t_var = rs1("t_var"); t_sd = rs1("t_sd"); t_rstd = rs1("t_rstd")
        t_nmr = rs1("t_nmr"); t_amax = rs1("t_amax"); t_rec = rs1("t_rec")
        t_r127 = rs1("t_r127"); t_sg = rs2("t_sg"); t_su = rs2("t_su")
        t_amxh = rs1("t_amxh"); t_rech = rs1("t_rech")
        t_r127h = rs1("t_r127h"); t_dqh = rs2("t_dqh")

        wld = [H_s, x_sb[0]]
        wabs = x_sb[1]
        wsgn_ofs = 2048
        wqbf = XqT_s
        wpart = qs_y[:, 64:96]
        t_tot = qs_y[:, 96:97]
        pe_chunk = qs_y[:, 0:64]

        sems = {e: ctx.enter_context(nc.semaphore(f"sem_{e}"))
                for e in ENGINES}
        block = ctx.enter_context(nc.Block())

        ring_f = {"evac": None}
        ring_b = {"evac": None}

        def ccopy(eng, out, in_):
            if hasattr(eng, "tensor_copy"):
                return eng.tensor_copy(out, in_)
            return eng.copy(out, in_)

        def dma(dst, src):
            def fn(eng):
                return eng.dma_start(out=dst, in_=src)
            return fn

        GPC = E("gpsimd")

        def gp_consts(g):
            g.memset(c_negD[:], -1.0 / D)
            g.memset(c_recD[:], 1.0 / D)
            g.memset(c_eps[:], 1e-5)
            g.memset(c_127[:], 127.0)
            g.memset(c_half[:], 0.5)
            g.memset(c_nhalf[:], -0.5)
            g.memset(c_1_127[:], 1.0 / 127.0)
            g.memset(c_wrec[:], 1.0 / (INNER * D))
            g.memset(onesP[:], 1.0)
            g.memset(S_sb[:], 0.0)
            g.memset(outhT_s[64:65, :], 1.0)
            g.memset(mask_keep[:], 0.0)
            g.affine_select(out=mask_keep[:], in_=mask_keep[:],
                            compare_op=Alu.is_gt, fill=1.0, base=0,
                            pattern=[[-1, 128]], channel_multiplier=1)
            g.memset(ident_f[:], 0.0)
            g.affine_select(out=ident_f[:], in_=ident_f[:],
                            compare_op=Alu.not_equal, fill=1.0, base=0,
                            pattern=[[-1, 128]], channel_multiplier=1)
            g.memset(ident_b[:], 0.0)
            return g.affine_select(out=ident_b[:], in_=ident_b[:],
                                   compare_op=Alu.not_equal, fill=1.0, base=0,
                                   pattern=[[-1, 128]], channel_multiplier=1)
        A("gpsimd", gp_consts, inc=GPC)

        LD_qw = E("sync", 16)
        A("sync", dma(wld[0][:, 0:1024], qkvw_d[0:128, :]), inc=LD_qw)
        LD_qw2 = E("sync", 16)
        A("sync", dma(wld[1][0:64, 0:1024], qkvw_d[128:192, :]), inc=LD_qw2)
        LD_qb = E("sync", 16)
        A("sync", dma(qb_qk[:], qkvb_d[0:128].rearrange("(p o) -> p o", o=1)),
          inc=LD_qb)
        LD_qbv = E("sync", 16)
        A("sync", dma(qb_vrow[:], qkvb_d[128:192].rearrange("(o d) -> o d", o=1)),
          inc=LD_qbv)
        LD_qbk = E("sync", 16)
        A("sync", dma(qb_krow[:], qkvb_d[64:128].rearrange("(o d) -> o d", o=1)),
          inc=LD_qbk)
        LD_pb = E("sync", 16)
        A("sync", dma(projT[64:65, :], projb_d.rearrange("(o d) -> o d", o=1)),
          inc=LD_pb)

        def f32_transposes(pieces, extra_first_wait=(), evac_eng="vector",
                           merged_out=None):
            last = None
            i = 0
            bi = 0
            while i < len(pieces):
                batch = pieces[i:i + 4]
                pe_ev = E("tensor")
                for sidx, (in_ap, out_ap, np_) in enumerate(batch):
                    def fn(eng, in_ap=in_ap, sidx=sidx, np_=np_):
                        dst = b7[0:in_ap.shape[1],
                                 sidx * 128: sidx * 128 + in_ap.shape[0]]
                        return eng.transpose(dst, in_ap,
                                             ident_f[0:np_, 0:np_])
                    w = []
                    if sidx == 0:
                        w = [ring_f["evac"], GPC] + list(extra_first_wait)
                    A("tensor", fn, waits=w,
                      inc=(pe_ev if sidx == len(batch) - 1 else None))
                ve = E(evac_eng)
                if merged_out is not None:
                    mo = merged_out[bi]
                    wdt = batch[0][0].shape[0]

                    def fn(eng, mo=mo, nb=len(batch), wdt=wdt,
                           np2=batch[0][0].shape[1]):
                        src = b7[0:np2, 0:nb * 128].rearrange(
                            "p (s i) -> p s i", i=128)[:, :, 0:wdt]
                        return ccopy(eng, mo, src)
                    A(evac_eng, fn, waits=[pe_ev], inc=ve)
                else:
                    for sidx, (in_ap, out_ap, np_) in enumerate(batch):
                        def fn(eng, in_ap=in_ap, out_ap=out_ap, sidx=sidx):
                            src = b7[0:in_ap.shape[1],
                                     sidx * 128: sidx * 128 + in_ap.shape[0]]
                            return ccopy(eng, out_ap, src)
                        A(evac_eng, fn, waits=([pe_ev] if sidx == 0 else ()),
                          inc=(ve if sidx == len(batch) - 1 else None))
                ring_f["evac"] = ve
                last = ve
                i += 4
                bi += 1
            return last

        A("tensor", None, waits=[LD_qw, LD_qw2])
        pieces = [(wld[0][:, k * 128:(k + 1) * 128],
                   qkvwT[:, k * 192: k * 192 + 128], 128) for k in range(8)]
        pieces += [(wld[1][0:64, k * 128:(k + 1) * 128],
                    qkvwT[:, k * 192 + 128: k * 192 + 192], 64)
                   for k in range(8)]
        EV_qkvwT = f32_transposes(pieces)

        prev_tr = [None, EV_qkvwT]
        for c in range(8):
            ld = E("sync", 16)
            w = [prev_tr[c % 2]] if prev_tr[c % 2] is not None else []
            if c < 2:
                w.append(EV_qkvwT)
            A("sync", dma(wld[c % 2][:, 0:1024],
                          projw_d[c * 128:(c + 1) * 128, :]),
              waits=w, inc=ld)
            rv = E("vector")

            def fn(eng, c=c):
                src = wld[c % 2][:, 0:1024].rearrange("p (r d) -> p d r", d=64)
                dst = pe_chunk[:].rearrange("p (d o) -> p d o", o=1)
                return eng.reduce_sum(dst, src, axis=AX.X)
            A("vector", fn, waits=[ld], inc=rv)
            ev = f32_transposes(
                [(pe_chunk[:, 0:64], projT[0:64, c * 128:(c + 1) * 128], 128)],
                extra_first_wait=[rv])
            prev_tr[c % 2] = ev
        EV_projT = ev

        def weight_prologue(w_dram, nchunks, ccols, dstT, dpc, wsc, wm127,
                            prev_done):
            njt = ccols // 128
            rvs = []
            for c in range(nchunks):
                ld = E("sync", 16)
                w = []
                if c == 0 and prev_done is not None:
                    w.append(prev_done)
                if c >= 2:
                    w.append(rvs[c - 2])
                A("sync", dma(wld[c % 2][:, 0:ccols],
                              w_dram[c * 128:(c + 1) * 128, :]),
                  waits=w, inc=ld)
                rv = E("vector")

                def fn(eng, c=c, ccols=ccols):
                    return eng.reduce_sum(wpart[:, c:c + 1],
                                          wld[c % 2][:, 0:ccols],
                                          axis=AX.X, apply_absolute_value=True)
                A("vector", fn, waits=[ld, GPC], inc=rv)
                rvs.append(rv)
            rv2 = E("vector")

            def fn(eng, nchunks=nchunks):
                return eng.reduce_sum(t_tot[:], wpart[:, 0:nchunks], axis=AX.X)
            A("vector", fn, waits=[rvs[-1]], inc=rv2)
            pe = E("tensor")

            def fn(eng):
                return eng.matmul(b0[:, 0:1], onesP[:], t_tot[:],
                                  start=True, stop=True)
            A("tensor", fn, waits=[rv2], inc=pe)
            vsc = E("vector")

            def fn(eng, wsc=wsc, wm127=wm127):
                eng.tensor_copy(t_tot[:], b0[:, 0:1])
                eng.tensor_single_scalar(t_tot[:], t_tot[:], c_wrec[:], Alu.mult)
                eng.tensor_single_scalar(t_tot[:], t_tot[:], c_eps[:], Alu.max)
                eng.tensor_single_scalar(wm127[:], t_tot[:], c_1_127[:],
                                         Alu.mult)
                return eng.reciprocal(wsc[:], t_tot[:])
            A("vector", fn, waits=[pe], inc=vsc)
            prev_sa = [None, None]
            prev_vq = None
            prev_pe_last = None
            ev_done = None
            hw = min(ccols, 2048)
            nhalf = ccols // hw
            for c in range(nchunks):
                ld = E("sync", 16)
                w = [vsc] if c == 0 else []
                if prev_sa[c % 2] is not None:
                    w.append(prev_sa[c % 2])
                A("sync", dma(wld[c % 2][:, 0:ccols],
                              w_dram[c * 128:(c + 1) * 128, :]),
                  waits=w, inc=ld)
                for h in range(nhalf):
                    sa = E("scalar")

                    def fn(eng, c=c, h=h, hw=hw, wsc=wsc):
                        eng.activation(out=wabs[:, 0:hw],
                                       in_=wld[c % 2][:, h * hw:(h + 1) * hw],
                                       func=Act.Abs, bias=0.0,
                                       scale=wsc[:, 0:1])
                        return eng.activation(
                            out=wabs[:, wsgn_ofs:wsgn_ofs + hw],
                            in_=wld[c % 2][:, h * hw:(h + 1) * hw],
                            func=Act.Sign, bias=0.0, scale=1.0)
                    ws_ = [ld, vsc]
                    if prev_vq is not None:
                        ws_.append(prev_vq)
                    A("scalar", fn, waits=ws_, inc=sa)
                    if h == nhalf - 1:
                        prev_sa[c % 2] = sa
                    vq = E("vector")

                    def fn(eng, hw=hw):
                        eng.tensor_single_scalar(wabs[:, 0:hw], wabs[:, 0:hw],
                                                 c_half[:], Alu.is_ge)
                        return eng.tensor_tensor(
                            wqbf[:, 0:hw], wabs[:, 0:hw],
                            wabs[:, wsgn_ofs:wsgn_ofs + hw], Alu.mult)
                    wv = [sa]
                    if prev_pe_last is not None:
                        wv.append(prev_pe_last)
                    A("vector", fn, waits=wv, inc=vq)
                    prev_vq = vq
                    njt_h = hw // 128
                    nbatch = max(1, njt_h // 8)
                    for bb in range(nbatch):
                        jlo = bb * 8
                        jn = min(8, njt_h - jlo)
                        pe_ev = E("tensor")
                        for sidx in range(jn):
                            jj = jlo + sidx

                            def fn(eng, jj=jj, sidx=sidx):
                                return eng.transpose(
                                    b6[:, sidx * 128:(sidx + 1) * 128],
                                    wqbf[:, jj * 128:(jj + 1) * 128],
                                    ident_b[:])
                            w = []
                            if sidx == 0:
                                w = [vq]
                                if ring_b["evac"] is not None:
                                    w.append(ring_b["evac"])
                            A("tensor", fn, waits=w,
                              inc=(pe_ev if sidx == jn - 1 else None))
                        prev_pe_last = pe_ev
                        ve = E("vector")
                        for sidx in range(jn):
                            j = h * njt_h + jlo + sidx

                            def fn(eng, j=j, sidx=sidx, c=c, dstT=dstT,
                                   dpc=dpc):
                                return eng.tensor_copy(
                                    dstT[:, j * dpc + c * 128:
                                         j * dpc + (c + 1) * 128],
                                    b6[:, sidx * 128:(sidx + 1) * 128])
                            A("vector", fn,
                              waits=([pe_ev] if sidx == 0 else ()),
                              inc=(ve if sidx == jn - 1 else None))
                        ring_b["evac"] = ve
                        ev_done = ve
            return ev_done

        EV_wg = weight_prologue(wg_d, 32, D, WgT8, INNER, wsc_g, wgm127,
                                EV_projT)
        EV_wu = weight_prologue(wu_d, 32, D, WuT8, INNER, wsc_u, wum127, EV_wg)
        EV_wd = weight_prologue(wd_d, 8, INNER, WdT8, D, wsc_d, wdm127, EV_wu)
        EV_codes = EV_wd

        XL = [None] * NG
        EO = [None] * NG
        EV_fin = [None] * (4 * NG)
        EP_state_g = [None] * NG
        prev_xnT_evac = [None]
        prev_vtok_pe = [None]
        prev_down_pe = [None]
        prev_hqT_pe = [None]
        prev_xqT_pe = [None]
        prev_intra3 = [None]
        prev_EVH_last2 = [None, None]
        Ag = [dict(xqTe=[None] * 4, qke=[None] * 4, vtok=[None] * 4,
                   ktoke=[None] * 4, guv=[None] * 4) for _ in range(NG)]

        def emit_xload(g):
            XL[g] = E("sync", 16)
            w = [EO[g - 2]] if g >= 2 else [EV_codes]
            src = x_d[g * GTOK:(g + 1) * GTOK, :].rearrange(
                "(c p) d -> p c d", p=128)
            dst = x_sb[g % 2][:].rearrange("p (c d) -> p c d", d=D)
            A("sync", dma(dst, src), waits=w, inc=XL[g])

        def emit_A_compute(g, c):
            par = g % 2
            k = 4 * g + c
            kp = k % 2
            xg = x_sb[par][:, c * D:(c + 1) * D]

            EV_negm = E("vector")

            def fn(eng, xg=xg, kp=kp):
                eng.reduce_sum(t_sum[kp][:], xg, axis=AX.X)
                return eng.tensor_single_scalar(
                    t_negm[kp][:], t_sum[kp][:], c_negD[:], Alu.mult)
            A("vector", fn, waits=[XL[g], GPC], inc=EV_negm)

            ES_ssq = E("scalar")

            def fn(eng, xg=xg, kp=kp):
                return eng.activation(out=xn[kp][:], in_=xg,
                                      func=Act.Square,
                                      bias=t_negm[kp][:], scale=1.0,
                                      accum_out=t_ssq[kp][:])
            A("scalar", fn, waits=[EV_negm, prev_xnT_evac[0]], inc=ES_ssq)

            EV_var = E("vector")

            def fn(eng, kp=kp):
                eng.tensor_single_scalar(t_var[kp][:], t_ssq[kp][:],
                                         c_recD[:], Alu.mult)
                return eng.tensor_single_scalar(
                    t_var[kp][:], t_var[kp][:], c_eps[:], Alu.add)
            A("vector", fn, waits=[ES_ssq], inc=EV_var)

            ES_sd = E("scalar")

            def fn(eng, kp=kp):
                return eng.activation(out=t_sd[kp][:], in_=t_var[kp][:],
                                      func=Act.Sqrt, bias=0.0, scale=1.0)
            A("scalar", fn, waits=[EV_var], inc=ES_sd)

            EV_rf = E("vector")

            def fn(eng, kp=kp):
                return eng.reciprocal(t_rstd[kp][:], t_sd[kp][:])
            A("vector", fn, waits=[ES_sd], inc=EV_rf)
            EV_nmr = E("vector")

            def fn(eng, kp=kp):
                return eng.tensor_single_scalar(
                    t_nmr[kp][:], t_negm[kp][:], t_rstd[kp][:], Alu.mult)
            A("vector", fn, waits=[EV_rf], inc=EV_nmr)

            ES_xn = E("scalar")

            def fn(eng, xg=xg, kp=kp):
                return eng.activation(out=xn[kp][:], in_=xg,
                                      func=Act.Identity,
                                      bias=t_nmr[kp][:],
                                      scale=t_rstd[kp][:])
            A("scalar", fn, waits=[EV_nmr], inc=ES_xn)

            EV_r127f = E("vector")

            def fn(eng, kp=kp):
                eng.reduce_max(t_amax[kp][:], xn[kp][:], axis=AX.X,
                               apply_absolute_value=True)
                eng.tensor_single_scalar(t_amax[kp][:], t_amax[kp][:],
                                         c_eps[:], Alu.max)
                eng.reciprocal(t_rec[kp][:], t_amax[kp][:])
                return eng.tensor_single_scalar(
                    t_r127[kp][:], t_rec[kp][:], c_127[:], Alu.mult)
            A("vector", fn, waits=[ES_xn], inc=EV_r127f)

            EV_gu = E("vector")

            def fn(eng, kp=kp):
                eng.tensor_single_scalar(t_sg[kp][:], t_amax[kp][:],
                                         wgm127[:], Alu.mult)
                return eng.tensor_single_scalar(
                    t_su[kp][:], t_amax[kp][:], wum127[:], Alu.mult)
            A("vector", fn, waits=[EV_r127f, EV_codes], inc=EV_gu)
            Ag[g]["guv"][c] = EV_gu

            EV_xq = E("vector")
            for sl in range(2):
                def fn(eng, sl=sl, kp=kp):
                    src = xn[kp][:, sl * 512:(sl + 1) * 512]
                    eng.tensor_single_scalar(qs_y[:], src,
                                             t_r127[kp][:], Alu.mult)
                    eng.tensor_copy(qs_ci[:], qs_y[:])
                    eng.scalar_tensor_tensor(qs_f1[:], qs_ci[:], c_half[:],
                                             qs_y[:], Alu.add, Alu.is_le)
                    eng.scalar_tensor_tensor(qs_y[:], qs_ci[:],
                                             c_nhalf[:], qs_y[:],
                                             Alu.add, Alu.is_ge)
                    eng.tensor_sub(qs_f1[:], qs_f1[:], qs_y[:])
                    return eng.tensor_tensor(
                        Xq_st[:, sl * 512:(sl + 1) * 512],
                        qs_f1[:], qs_ci[:], Alu.add)
                w = []
                if sl == 0:
                    w = [EV_gu]
                    if prev_xqT_pe[0] is not None:
                        w.append(prev_xqT_pe[0])
                    if prev_intra3[0] is not None:
                        w.append(prev_intra3[0])
                A("vector", fn, waits=w,
                  inc=(EV_xq if sl == 1 else None))
            return ES_xn, EV_xq

        def emit_A_pe1(g, c, ES_xn):
            k = 4 * g + c
            kp = k % 2
            A("tensor", None, waits=[ES_xn])
            ev = f32_transposes(
                [(xn[kp][:, j * 128:(j + 1) * 128],
                  xnT[kp][:, j * 128:(j + 1) * 128], 128)
                 for j in range(8)],
                extra_first_wait=([prev_vtok_pe[0]]
                                  if prev_vtok_pe[0] is not None else []),
                evac_eng="vector")
            prev_xnT_evac[0] = ev

        def emit_A_pe2(g, c, gu_gate_ev, EV_xq):
            par = g % 2
            k = 4 * g + c
            kp = k % 2
            pe_ev = E("tensor")
            for sidx in range(8):
                def fn(eng, sidx=sidx):
                    return eng.transpose(
                        b6[:, sidx * 128:(sidx + 1) * 128],
                        Xq_st[:, sidx * 128:(sidx + 1) * 128], ident_b[:])
                w = []
                if sidx == 0:
                    w = [EV_xq]
                    if ring_b["evac"] is not None:
                        w.append(ring_b["evac"])
                A("tensor", fn, waits=w, inc=(pe_ev if sidx == 7 else None))
            prev_xqT_pe[0] = pe_ev
            ve = E("scalar")

            def fn(eng, par=par, c=c):
                dst = XqT_sb[par][:].rearrange(
                    "p (s t) -> p s t", t=GTOK)[:, :, c * 128:(c + 1) * 128]
                src = b6[:].rearrange("p (s i) -> p s i", i=128)
                return ccopy(eng, dst, src)
            w = [pe_ev]
            if gu_gate_ev is not None:
                w.append(gu_gate_ev)
            A("scalar", fn, waits=w, inc=ve)
            ring_b["evac"] = ve
            Ag[g]["xqTe"][c] = ve

            pe_ev = E("tensor")
            for kk in range(8):
                def fn(eng, kk=kk, kp=kp):
                    return eng.matmul(
                        b7[:, 0:128],
                        qkvwT[:, kk * 192: kk * 192 + 128],
                        xnT[kp][:, kk * 128:(kk + 1) * 128],
                        start=(kk == 0), stop=(kk == 7))
                w = []
                if kk == 0:
                    w = [ring_f["evac"], EV_qkvwT]
                A("tensor", fn, waits=w, inc=(pe_ev if kk == 7 else None))
            ve = E("scalar")

            def fn(eng, par=par, c=c):
                eng.activation(out=q_sb[par][0:64, c * 128:(c + 1) * 128],
                               in_=b7[0:64, 0:128], func=Act.Identity,
                               bias=qb_qk[0:64, 0:1], scale=1.0)
                return eng.activation(
                    out=kT_sb[par][0:64, c * 128:(c + 1) * 128],
                    in_=b7[64:128, 0:128], func=Act.Identity,
                    bias=qb_qk[64:128, 0:1], scale=1.0)
            A("scalar", fn, waits=[pe_ev, LD_qb], inc=ve)
            ring_f["evac"] = ve
            Ag[g]["qke"][c] = ve

            pe_v = E("tensor")
            for kk in range(9):
                def fn(eng, kk=kk, kp=kp):
                    dst = b7[:, 0:64]
                    if kk < 8:
                        return eng.matmul(
                            dst, xnT[kp][:, kk * 128:(kk + 1) * 128],
                            qkvwT[:, kk * 192 + 128: kk * 192 + 192],
                            start=(kk == 0), stop=False)
                    return eng.matmul(dst, ones_row, qb_vrow[:],
                                      start=False, stop=True)
                w = []
                if kk == 0:
                    w = [ring_f["evac"], LD_qbv]
                A("tensor", fn, waits=w, inc=(pe_v if kk == 8 else None))
            pe_k = E("tensor")
            for kk in range(9):
                def fn(eng, kk=kk, kp=kp):
                    dst = b7[:, 64:128]
                    if kk < 8:
                        return eng.matmul(
                            dst, xnT[kp][:, kk * 128:(kk + 1) * 128],
                            qkvwT[:, kk * 192 + 64: kk * 192 + 128],
                            start=(kk == 0), stop=False)
                    return eng.matmul(dst, ones_row, qb_krow[:],
                                      start=False, stop=True)
                w = [LD_qbk] if kk == 0 else []
                A("tensor", fn, waits=w, inc=(pe_k if kk == 8 else None))
            prev_vtok_pe[0] = pe_k
            ve = E("scalar")

            def fn(eng, par=par, c=c):
                ccopy(eng, v_sb[par][:, c * HS:(c + 1) * HS], b7[:, 0:64])
                return ccopy(eng, k_sb[par][:, c * HS:(c + 1) * HS],
                             b7[:, 64:128])
            A("scalar", fn, waits=[pe_v, pe_k], inc=ve)
            ring_f["evac"] = ve
            Ag[g]["vtok"][c] = ve
            Ag[g]["ktoke"][c] = ve

        def emit_B_front(g, prev_silu7):
            par = g % 2
            EP_inter = E("tensor")

            def fn(eng, par=par):
                return eng.matmul(b4[0:64, 0:GTOK], S_sb[:], q_sb[par][:],
                                  start=True, stop=False)
            w = [Ag[g]["qke"][3]]
            if EV_S_prev_box[0] is not None:
                w.append(EV_S_prev_box[0])
            if prev_silu7 is not None:
                w.append(prev_silu7)
            A("tensor", fn, waits=w, inc=EP_inter)

            EV_wei = [None] * 4
            EP_intra = [None] * 4
            for c in range(4):
                bank = b2 if c % 2 == 0 else b3
                ncols = GTOK - c * 128
                pe_ev = E("tensor")

                def fn(eng, c=c, bank=bank, ncols=ncols, par=par):
                    return eng.matmul(bank[:, 0:ncols],
                                      kT_sb[par][0:64, c * 128:(c + 1) * 128],
                                      q_sb[par][0:64, c * 128:GTOK],
                                      start=True, stop=True)
                w = [Ag[g]["qke"][c]]
                if c >= 2:
                    w.append(EV_wei[c - 2])
                elif g > 0:
                    w.append(prev_EVH_last2[c])
                A("tensor", fn, waits=w, inc=pe_ev)
                ve = E("vector")

                def fn(eng, c=c, bank=bank, ncols=ncols):
                    wsl = wei_sb[c % 2]
                    last = eng.tensor_tensor(wsl[:, 0:128], bank[:, 0:128],
                                             mask_keep[:], Alu.mult)
                    if ncols > 128:
                        last = eng.tensor_copy(wsl[:, 128:ncols],
                                               bank[:, 128:ncols])
                    return last
                w = [pe_ev]
                if c >= 1:
                    w.append(EP_intra[c - 1])
                A("vector", fn, waits=w, inc=ve)
                EV_wei[c] = ve
                pe_ev = E("tensor")

                def fn(eng, c=c, ncols=ncols, par=par):
                    return eng.matmul(b4[0:64, c * 128:GTOK],
                                      v_sb[par][:, c * HS:(c + 1) * HS],
                                      wei_sb[c % 2][:, 0:ncols],
                                      start=False, stop=(c == 3))
                A("tensor", fn, waits=[ve, Ag[g]["vtok"][c]], inc=pe_ev)
                EP_intra[c] = pe_ev
            prev_intra3[0] = EP_intra[3]

            EP_state = E("tensor")
            for c in range(4):
                def fn(eng, c=c, par=par):
                    return eng.matmul(b5[0:64, 0:64],
                                      k_sb[par][:, c * HS:(c + 1) * HS],
                                      v_sb[par][:, c * HS:(c + 1) * HS],
                                      start=(c == 0), stop=(c == 3))
                w = []
                if c == 0:
                    w = [Ag[g]["ktoke"][3], EP_inter]
                    if g > 0:
                        w.append(prev_EVH_last2[1])
                A("tensor", fn, waits=w, inc=(EP_state if c == 3 else None))
            EP_state_g[g] = EP_state

            EV_S_upd = E("vector")

            def fn(eng):
                return eng.tensor_tensor(S_sb[:], S_sb[:], b5[0:64, 0:64],
                                         Alu.add)
            A("vector", fn, waits=[EP_state], inc=EV_S_upd)
            EV_S_prev_box[0] = EV_S_upd

            EV_outh = E("scalar")

            def fn(eng, par=par):
                return ccopy(eng, outhT[par][0:64, :], b4[0:64, :])
            A("scalar", fn, waits=[EP_intra[3]], inc=EV_outh)
            return EV_wei, EV_outh, EV_S_upd

        def emit_B_proj(g, EV_outh):
            par = g % 2
            EV_xp = [None] * 4
            for c in range(4):
                pe_ev = E("tensor")

                def fn(eng, c=c, par=par):
                    eng.matmul(b0[:], outhT[par][:, c * 128:(c + 1) * 128],
                               projT[:, 0:512], start=True, stop=True)
                    return eng.matmul(b1[:],
                                      outhT[par][:, c * 128:(c + 1) * 128],
                                      projT[:, 512:1024],
                                      start=True, stop=True)
                w = [EV_outh, EV_projT, LD_pb]
                if c == 0:
                    if g > 0:
                        w.append(EV_fin[4 * g - 1])
                else:
                    w.append(EV_xp[c - 1])
                A("tensor", fn, waits=w, inc=pe_ev)
                ve = E("vector")

                def fn(eng, c=c, par=par):
                    xs = x_sb[par]
                    eng.tensor_tensor(xs[:, c * D: c * D + 512],
                                      xs[:, c * D: c * D + 512], b0[:],
                                      Alu.add)
                    return eng.tensor_tensor(xs[:, c * D + 512:(c + 1) * D],
                                             xs[:, c * D + 512:(c + 1) * D],
                                             b1[:], Alu.add)
                A("vector", fn, waits=[pe_ev], inc=ve)
                EV_xp[c] = ve
            return EV_xp

        EV_S_prev_box = [None]


        emit_xload(0)
        for c in range(4):
            _esxn, _evxq = emit_A_compute(0, c)
            emit_A_pe1(0, c, _esxn)
            emit_A_pe2(0, c, None, _evxq)
        prev_silu7_box = [None]
        _w, _o, _su = emit_B_front(0, None)
        _xp = emit_B_proj(0, _o)
        Bg = {0: (_w, _o, _su, _xp)}

        for g in range(NG):
            par = g % 2
            EV_wei, EV_outh, EV_S_upd, EV_xp = Bg[g]
            if g + 1 < NG:
                emit_xload(g + 1)

            def flush_hqT(pk, pc, hq_evs):
                kpp = pk % 2
                hqTe_last = None
                for bb in range(4):
                    pe_ev = E("tensor")
                    for sidx in range(8):
                        j = bb * 8 + sidx

                        def fn(eng, j=j, sidx=sidx, kpp=kpp):
                            return eng.transpose(
                                b6[:, sidx * 128:(sidx + 1) * 128],
                                Hq[kpp][:, j * 128:(j + 1) * 128],
                                ident_b[:])
                        w = []
                        if sidx == 0:
                            w = [hq_evs[2 * bb + 1]]
                            if ring_b["evac"] is not None:
                                w.append(ring_b["evac"])
                        A("tensor", fn, waits=w,
                          inc=(pe_ev if sidx == 7 else None))
                    if bb == 3:
                        prev_hqT_pe[0] = pe_ev
                    ve = E("scalar")

                    def fn(eng, bb=bb):
                        return ccopy(eng,
                                     HqT_s[:, bb * 1024:(bb + 1) * 1024],
                                     b6[:])
                    w = [pe_ev]
                    if bb == 0 and prev_down_pe[0] is not None:
                        w.append(prev_down_pe[0])
                    A("scalar", fn, waits=w, inc=ve)
                    ring_b["evac"] = ve
                    hqTe_last = ve
                return hqTe_last

            def flush_down(pk, pc, hqTe_last, dqh_ev):
                EP_down = E("tensor")
                for j in range(32):
                    for n2 in range(2):
                        bank = b0 if n2 == 0 else b1

                        def fn(eng, j=j, bank=bank, n2=n2):
                            return eng.matmul(
                                bank[:],
                                HqT_s[:, j * 128:(j + 1) * 128],
                                WdT8[:, j * D + n2 * 512:
                                     j * D + (n2 + 1) * 512],
                                start=(j == 0), stop=(j == 31))
                        w = []
                        if j == 0 and n2 == 0:
                            w = [hqTe_last, EV_xp[3]]
                            if pk > 0 and EV_fin[pk - 1] is not None:
                                w.append(EV_fin[pk - 1])
                        A("tensor", fn, waits=w,
                          inc=(EP_down if (j == 31 and n2 == 1) else None))
                prev_down_pe[0] = EP_down
                fin = E("vector")

                def fn(eng, pc=pc, par=par, pk=pk):
                    xs = x_sb[par]
                    kpp = pk % 2
                    eng.scalar_tensor_tensor(
                        xs[:, pc * D: pc * D + 512], b0[:],
                        t_dqh[kpp][:], xs[:, pc * D: pc * D + 512],
                        Alu.mult, Alu.add)
                    return eng.scalar_tensor_tensor(
                        xs[:, pc * D + 512:(pc + 1) * D], b1[:],
                        t_dqh[kpp][:],
                        xs[:, pc * D + 512:(pc + 1) * D],
                        Alu.mult, Alu.add)
                A("vector", fn, waits=[EP_down, dqh_ev], inc=fin)
                EV_fin[pk] = fin

            pend = None
            hqTe_pend = None
            for c in range(4):
                k = 4 * g + c
                kp = k % 2
                if g + 1 < NG:
                    a_es_xn, a_ev_xq = emit_A_compute(g + 1, c)
                EV_H = [None] * 8
                gu_chunk_last = None
                for n in range(8):
                    Gb = b2 if n % 2 == 0 else b4
                    Ub = b3 if n % 2 == 0 else b5
                    pe_ev = E("tensor")
                    for kk in range(8):
                        def fn(eng, kk=kk, n=n, Gb=Gb, Ub=Ub, par=par, c=c):
                            lhs = XqT_sb[par][:, kk * GTOK + c * 128:
                                              kk * GTOK + (c + 1) * 128]
                            eng.matmul(Gb[:], lhs,
                                       WgT8[:, kk * INNER + n * 512:
                                            kk * INNER + (n + 1) * 512],
                                       start=(kk == 0), stop=(kk == 7))
                            return eng.matmul(
                                Ub[:], lhs,
                                WuT8[:, kk * INNER + n * 512:
                                     kk * INNER + (n + 1) * 512],
                                start=(kk == 0), stop=(kk == 7))
                        w = []
                        if kk == 0:
                            w = [Ag[g]["xqTe"][c], EV_codes]
                            if n >= 2:
                                w.append(EV_H[n - 2])
                            elif c == 0:
                                if n == 0:
                                    w += [EV_wei[3], EV_wei[2]]
                                else:
                                    w += [EV_outh, EV_S_upd]
                            else:
                                w.append(prev_EVH_last2[n])
                        A("tensor", fn, waits=w,
                          inc=(pe_ev if kk == 7 else None))
                    gu_chunk_last = pe_ev
                    if n == 1 and pend is not None:
                        hqTe_pend = flush_hqT(pend[0], pend[1], pend[2])
                    if n == 2 and g + 1 < NG:
                        emit_A_pe1(g + 1, c, a_es_xn)

                    sa = E("scalar")

                    def fn(eng, Gb=Gb, kp=kp):
                        return eng.activation(out=g_act_s[:], in_=Gb[:],
                                              func=Act.Silu, bias=0.0,
                                              scale=t_sg[kp][:, 0:1])
                    w = [pe_ev, Ag[g]["guv"][c]]
                    if n >= 1:
                        w.append(EV_H[n - 1])
                    elif c > 0:
                        w.append(prev_EVH_last2[1])
                    A("scalar", fn, waits=w, inc=sa)
                    if n == 7:
                        prev_silu7_box[0] = sa
                    ve = E("vector")

                    def fn(eng, n=n, Ub=Ub, kp=kp):
                        return eng.scalar_tensor_tensor(
                            H[kp][:, n * 512:(n + 1) * 512], Ub[:],
                            t_su[kp][:, 0:1], g_act_s[:],
                            Alu.mult, Alu.mult)
                    A("vector", fn, waits=[sa], inc=ve)
                    EV_H[n] = ve
                    vp = E("vector")

                    def fn(eng, n=n, kp=kp):
                        return eng.reduce_max(
                            amx8[:, n:n + 1],
                            H[kp][:, n * 512:(n + 1) * 512],
                            axis=AX.X, apply_absolute_value=True)
                    A("vector", fn, waits=(), inc=vp)
                    if n == 7:
                        EV_amx_last = vp
                prev_EVH_last2 = [EV_H[6], EV_H[7]]

                EV_hf = E("vector")

                def fn(eng, kp=kp):
                    eng.reduce_max(t_amxh[kp][:], amx8[:], axis=AX.X)
                    eng.tensor_single_scalar(t_amxh[kp][:], t_amxh[kp][:],
                                             c_eps[:], Alu.max)
                    eng.reciprocal(t_rech[kp][:], t_amxh[kp][:])
                    return eng.tensor_single_scalar(
                        t_r127h[kp][:], t_rech[kp][:], c_127[:], Alu.mult)
                A("vector", fn, waits=[EV_amx_last], inc=EV_hf)
                EV_dqh = E("vector")

                def fn(eng, kp=kp):
                    return eng.tensor_single_scalar(
                        t_dqh[kp][:], t_amxh[kp][:], wdm127[:], Alu.mult)
                A("vector", fn, waits=[EV_hf], inc=EV_dqh)

                EV_hqs = [E("vector") for _ in range(8)]
                for sl in range(8):
                    def fn(eng, sl=sl, kp=kp):
                        hs = H[kp][:, sl * 512:(sl + 1) * 512]
                        eng.tensor_single_scalar(qs_y[:], hs, t_r127h[kp][:],
                                                 Alu.mult)
                        eng.tensor_copy(qs_ci[:], qs_y[:])
                        eng.scalar_tensor_tensor(qs_f1[:], qs_ci[:],
                                                 c_half[:], qs_y[:],
                                                 Alu.add, Alu.is_le)
                        eng.scalar_tensor_tensor(qs_y[:], qs_ci[:],
                                                 c_nhalf[:], qs_y[:],
                                                 Alu.add, Alu.is_ge)
                        eng.tensor_sub(qs_f1[:], qs_f1[:], qs_y[:])
                        return eng.tensor_tensor(
                            Hq[kp][:, sl * 512:(sl + 1) * 512],
                            qs_f1[:], qs_ci[:], Alu.add)
                    w = []
                    if sl == 0:
                        w = [EV_dqh]
                        if prev_hqT_pe[0] is not None:
                            w.append(prev_hqT_pe[0])
                    A("vector", fn, waits=w, inc=EV_hqs[sl])

                if g + 1 < NG:
                    emit_A_pe2(g + 1, c, gu_chunk_last, a_ev_xq)

                if pend is not None:
                    flush_down(pend[0], pend[1], hqTe_pend, pend[3])
                pend = (k, c, EV_hqs, EV_dqh)

            if g + 1 < NG:
                _w, _o, _su = emit_B_front(g + 1, prev_silu7_box[0])
            hqTe_pend = flush_hqT(pend[0], pend[1], pend[2])
            flush_down(pend[0], pend[1], hqTe_pend, pend[3])
            if g + 1 < NG:
                _xp = emit_B_proj(g + 1, _o)
                Bg[g + 1] = (_w, _o, _su, _xp)

            EO[g] = E("sync", 16)
            dstap = y_d[g * GTOK:(g + 1) * GTOK, :].rearrange(
                "(c p) d -> p c d", p=128)
            osrc = x_sb[par][:].rearrange("p (c d) -> p c d", d=D)
            A("sync", dma(dstap, osrc), waits=[EV_fin[4 * g + 3]],
              inc=EO[g])

        A("sync", None, waits=[EO[NG - 1]])

        p.assign()
        p.check()

        def emit(eng_obj, eng_name):
            for fn, waits, inc in p.ops[eng_name]:
                for w in waits:
                    eng_obj.wait_ge(sems[w.eng], w.val)
                instr = fn(eng_obj) if fn is not None else None
                if inc is not None:
                    instr.then_inc(sems[inc.eng], inc.amount)

        @block.sync
        def _(e):
            emit(e, "sync")

        @block.gpsimd
        def _(e):
            emit(e, "gpsimd")

        @block.vector
        def _(e):
            emit(e, "vector")

        @block.scalar
        def _(e):
            emit(e, "scalar")

        @block.tensor
        def _(e):
            emit(e, "tensor")

    return nc


_NC_CACHE = {}


def kernel(L0_attn_ids, L0_attn_qkv_w, L0_attn_qkv_b, L0_attn_proj_w,
           L0_attn_proj_b, L0_ffn_w_gate, L0_ffn_w_up, L0_ffn_w_down):
    x = np.ascontiguousarray(np.asarray(L0_attn_ids, dtype=np.float32))
    B = x.shape[0]
    if "nc" not in _NC_CACHE:
        _NC_CACHE["nc"] = build_program()
    nc = _NC_CACHE["nc"]
    common = {
        "qkv_w": np.ascontiguousarray(np.asarray(L0_attn_qkv_w, np.float32)),
        "qkv_b": np.ascontiguousarray(np.asarray(L0_attn_qkv_b, np.float32)),
        "proj_w": np.ascontiguousarray(np.asarray(L0_attn_proj_w, np.float32)),
        "proj_b": np.ascontiguousarray(np.asarray(L0_attn_proj_b, np.float32)),
        "w_gate": np.ascontiguousarray(np.asarray(L0_ffn_w_gate, np.float32)),
        "w_up": np.ascontiguousarray(np.asarray(L0_ffn_w_up, np.float32)),
        "w_down": np.ascontiguousarray(np.asarray(L0_ffn_w_down, np.float32)),
    }
    in_maps = [dict(common, x=np.ascontiguousarray(x[b])) for b in range(B)]
    res = run_bass_kernel_spmd(nc, in_maps, core_ids=list(range(B)))
    return np.stack([np.asarray(r["y"], dtype=np.float32)
                     for r in res.results], axis=0)


# version 11 (final)
# speedup vs baseline: 1.4531x; 1.0101x over previous
import numpy as np
import ml_dtypes
import concourse.bass as bass
import concourse.mybir as mybir
from concourse.bass_utils import run_bass_kernel_spmd

F32 = mybir.dt.float32
BF16 = mybir.dt.bfloat16
FP8 = mybir.dt.float8e4
I32 = mybir.dt.int32
I8 = mybir.dt.int8
Alu = mybir.AluOpType
Act = mybir.ActivationFunctionType
AX = mybir.AxisListType

T = 8192
D = 1024
HS = 64
INNER = 4096
NG = 16
GTOK = 512
ENGINES = ("sync", "gpsimd", "vector", "scalar", "tensor")


class Ev:
    __slots__ = ("eng", "amount", "val", "sem")

    def __init__(self, eng, amount=1, sem=None):
        self.eng = eng
        self.amount = amount
        self.val = None
        self.sem = sem or eng


class Prog:
    def __init__(self):
        self.ops = {e: [] for e in ENGINES}

    def add(self, eng, fn, waits=(), inc=None):
        ws = tuple(w for w in waits if w is not None)
        if inc is not None:
            assert inc.eng == eng
        self.ops[eng].append((fn, ws, inc))
        return inc

    def assign(self):
        c = {}
        for e in ENGINES:
            for fn, waits, inc in self.ops[e]:
                if inc is not None:
                    c[inc.sem] = c.get(inc.sem, 0) + inc.amount
                    inc.val = c[inc.sem]

    def check(self):
        cnt = {e: 0 for e in ENGINES}
        cnt["gpdma"] = 0
        idx = {e: 0 for e in ENGINES}
        total = sum(len(v) for v in self.ops.values())
        done = 0
        while done < total:
            progressed = False
            for e in ENGINES:
                while idx[e] < len(self.ops[e]):
                    fn, waits, inc = self.ops[e][idx[e]]
                    ok = True
                    for w in waits:
                        assert w.val is not None, f"unassigned event wait in {e}"
                        if cnt[w.sem] < w.val:
                            ok = False
                            break
                    if not ok:
                        break
                    if inc is not None:
                        cnt[inc.sem] = inc.val
                    idx[e] += 1
                    done += 1
                    progressed = True
            if not progressed:
                detail = []
                for e in ENGINES:
                    if idx[e] < len(self.ops[e]):
                        fn, waits, inc = self.ops[e][idx[e]]
                        bad = [(w.sem, w.val, cnt[w.sem]) for w in waits
                               if cnt[w.sem] < w.val]
                        detail.append(f"{e}@{idx[e]} blocked on {bad}")
                raise RuntimeError("DEADLOCK in schedule: " + "; ".join(detail))


def build_program():
    nc = bass.Bass("TRN2", target_bir_lowering=False)
    x_d = nc.dram_tensor("x", [T, D], F32, kind="ExternalInput").ap()
    qkvw_d = nc.dram_tensor("qkv_w", [D, 3 * HS], F32, kind="ExternalInput").ap()
    qkvb_d = nc.dram_tensor("qkv_b", [3 * HS], F32, kind="ExternalInput").ap()
    projw_d = nc.dram_tensor("proj_w", [64, 16 * D], F32,
                             kind="ExternalInput").ap()
    projb_d = nc.dram_tensor("proj_b", [D], F32, kind="ExternalInput").ap()
    wg_d = nc.dram_tensor("w_gate", [D, INNER], F32, kind="ExternalInput").ap()
    wu_d = nc.dram_tensor("w_up", [D, INNER], F32, kind="ExternalInput").ap()
    wd_d = nc.dram_tensor("w_down", [INNER, D], F32, kind="ExternalInput").ap()
    cf_d = nc.dram_tensor("consts_f", [128, 128], F32,
                          kind="ExternalInput").ap()
    cb_d = nc.dram_tensor("consts_b", [128, 256], BF16,
                          kind="ExternalInput").ap()
    y_d = nc.dram_tensor("y", [T, D], F32, kind="ExternalOutput").ap()

    from contextlib import ExitStack
    p = Prog()
    A = p.add

    def E(eng, amount=1):
        return Ev(eng, amount)

    def vchain(steps, first_waits=(), last_inc=None):
        prev = None
        for i, fn in enumerate(steps):
            last = i == len(steps) - 1
            ev = last_inc if last else Ev("vector", 1)
            A("vector", fn,
              waits=(list(first_waits) if i == 0 else [prev]), inc=ev)
            prev = ev
        return prev

    with ExitStack() as ctx:
        def sb(name, shape, dt=F32):
            return ctx.enter_context(nc.sbuf_tensor(name, shape, dt))

        def psum(name, shape, dt=F32):
            return ctx.enter_context(nc.psum_tensor(name, shape, dt))

        b0 = psum("b0", [128, 512])
        b1 = psum("b1", [128, 512])
        b2 = psum("b2", [128, 512])
        b3 = psum("b3", [128, 512])
        b4 = psum("b4", [128, 512])
        b5 = psum("b5", [128, 512])
        b6 = psum("b6", [128, 1024], BF16)
        b7 = psum("b7", [128, 512])

        WgT8 = sb("WgT8", [128, 8 * INNER], FP8)
        WuT8 = sb("WuT8", [128, 8 * INNER], FP8)
        WdT8 = sb("WdT8", [128, 32 * D], FP8)
        qkvwT = sb("qkvwT", [128, 8 * 192])
        projT = sb("projT", [65, D])
        qb_qk = sb("qb_qk", [128, 1])
        qb_vrow = sb("qb_vrow", [1, HS])
        qb_krow = sb("qb_krow", [1, HS])
        mask_keep = sb("mask_keep", [128, 128], BF16)
        ident_f = sb("ident_f", [128, 128])
        ident_b = sb("ident_b", [128, 128], BF16)
        onesP = sb("onesP", [128, 128])
        ones_row = onesP[0:1, 0:128]
        S_sb = sb("S_sb", [64, 64])
        c_blk = sb("c_blk", [128, 16])
        (c_negD, c_recD, c_eps, c_127, c_half, c_nhalf, c_1_127, c_wrec,
         wgm127, wum127, wdm127, wsc_g, wsc_u, wsc_d) = [
            c_blk[:, i:i + 1] for i in range(14)]

        x_sb = [sb(f"x_sb{i}", [128, 4 * D]) for i in range(2)]
        xn_s = sb("xn_s", [128, D])
        xnT_s = sb("xnT_s", [128, 8 * 128])
        XqT_s = sb("XqT_s", [128, 8 * GTOK], BF16)
        q_s = sb("q_s", [64, GTOK])
        kT_s = sb("kT_s", [64, GTOK])
        k_s = sb("k_s", [128, 4 * HS])
        v_s = sb("v_s", [128, 4 * HS])
        wei_s = sb("wei_s", [128, GTOK])
        wei_sb = [wei_s, wei_s]
        Xq_st = sb("Xq_st", [128, D], BF16)
        q_sb = [q_s, q_s]; kT_sb = [kT_s, kT_s]
        k_sb = [k_s, k_s]; v_sb = [v_s, v_s]
        outhT_s = sb("outhT_s", [65, GTOK])
        H_s = sb("H_s", [128, INNER])
        Hq_s = sb("Hq_s", [128, INNER], BF16)
        HqT_s = sb("HqT_s", [128, 32 * 128], BF16)
        g_act_s = sb("g_act_s", [128, 512])
        g_act = [g_act_s, g_act_s]
        qs_ye = sb("qs_ye", [128, 512])
        qs_ci8 = sb("qs_ci8", [128, 1024], I8)
        qs_f8 = sb("qs_f8", [128, 1024], BF16)
        xn = [xn_s, xn_s]
        xnT = [xnT_s, xnT_s]
        XqT_sb = [XqT_s, XqT_s]
        outhT = [outhT_s, outhT_s]
        H = [H_s, H_s]
        Hq = [Hq_s, Hq_s]
        HqT_sb = [HqT_s, HqT_s]

        rsb = sb("rsb", [128, 24])
        amx8 = rsb[:, 16:24]
        _rsn = [0]

        def rs1(name):
            t = rsb[:, _rsn[0]:_rsn[0] + 1]
            _rsn[0] += 1
            return [t, t]

        def rs2(name):
            return [rs1(name)[0], rs1(name)[0]]
        t_sum = rs1("t_sum"); t_negm = rs1("t_negm"); t_ssq = rs1("t_ssq")
        t_var = rs1("t_var"); t_sd = rs1("t_sd"); t_rstd = rs1("t_rstd")
        t_nmr = rs1("t_nmr"); t_amax = rs1("t_amax"); t_rec = rs1("t_rec")
        t_r127 = rs1("t_r127")
        t_sg8 = sb("t_sg8", [128, 8])
        t_su8 = sb("t_su8", [128, 8])
        t_sg = [t_sg8[:, i:i + 1] for i in range(8)]
        t_su = [t_su8[:, i:i + 1] for i in range(8)]
        t_amxh = rs1("t_amxh"); t_rech = rs1("t_rech")
        t_r127h = rs1("t_r127h"); t_dqh = rs2("t_dqh")

        wld4 = [H_s[:, i * 1024:(i + 1) * 1024] for i in range(4)]
        y_pp = [x_sb[1][:, 0:1024], x_sb[1][:, 1024:2048]]
        t_le_pp = [x_sb[1][:, 2048:3072], x_sb[1][:, 3072:4096]]
        wpart = qs_ye[:, 64:96]
        t_tot = qs_ye[:, 96:97]

        sems = {e: ctx.enter_context(nc.semaphore(f"sem_{e}"))
                for e in ENGINES + ("gpdma",)}
        block = ctx.enter_context(nc.Block())

        ring_f = {"evac": None}
        ring_b = {"evac": None}

        def ccopy(eng, out, in_):
            if hasattr(eng, "tensor_copy"):
                return eng.tensor_copy(out, in_)
            return eng.copy(out, in_)

        def dma(dst, src):
            def fn(eng):
                return eng.dma_start(out=dst, in_=src)
            return fn

        GPC = E("gpsimd")

        def gp_consts(g):
            g.memset(c_negD[:], -1.0 / D)
            g.memset(c_recD[:], 1.0 / D)
            g.memset(c_eps[:], 1e-5)
            g.memset(c_127[:], 127.0)
            g.memset(c_half[:], 0.5)
            g.memset(c_nhalf[:], -0.5)
            g.memset(c_1_127[:], 1.0 / 127.0)
            g.memset(c_wrec[:], 1.0 / (INNER * D))
            g.memset(onesP[:], 1.0)
            g.memset(S_sb[:], 0.0)
            return g.memset(outhT_s[64:65, :], 1.0)
        A("gpsimd", gp_consts, inc=GPC)
        LD_CF = E("sync", 16)
        A("sync", dma(ident_f[:], cf_d[:]), inc=LD_CF)
        LD_CB = E("sync", 16)
        A("sync", dma(ident_b[:], cb_d[:, 0:128]), inc=LD_CB)
        A("sync", dma(mask_keep[:], cb_d[:, 128:256]), inc=LD_CB)

        LD_qb = E("sync", 16)
        A("sync", dma(qb_qk[:], qkvb_d[0:128].rearrange("(p o) -> p o", o=1)),
          inc=LD_qb)
        LD_qbv = E("sync", 16)
        A("sync", dma(qb_vrow[:], qkvb_d[128:192].rearrange("(o d) -> o d", o=1)),
          inc=LD_qbv)
        LD_qbk = E("sync", 16)
        A("sync", dma(qb_krow[:], qkvb_d[64:128].rearrange("(o d) -> o d", o=1)),
          inc=LD_qbk)
        LD_pb = E("sync", 16)
        A("sync", dma(projT[64:65, :], projb_d.rearrange("(o d) -> o d", o=1)),
          inc=LD_pb)

        def f32_transposes(pieces, extra_first_wait=(), evac_eng="vector",
                           merged_out=None):
            last = None
            i = 0
            bi = 0
            while i < len(pieces):
                batch = pieces[i:i + 4]
                pe_ev = E("tensor")
                for sidx, (in_ap, out_ap, np_) in enumerate(batch):
                    def fn(eng, in_ap=in_ap, sidx=sidx, np_=np_):
                        dst = b7[0:in_ap.shape[1],
                                 sidx * 128: sidx * 128 + in_ap.shape[0]]
                        return eng.transpose(dst, in_ap,
                                             ident_f[0:np_, 0:np_])
                    w = []
                    if sidx == 0:
                        w = [ring_f["evac"], GPC, LD_CF] + list(extra_first_wait)
                    A("tensor", fn, waits=w,
                      inc=(pe_ev if sidx == len(batch) - 1 else None))
                ve = E(evac_eng)
                if merged_out is not None:
                    mo = merged_out[bi]
                    wdt = batch[0][0].shape[0]

                    def fn(eng, mo=mo, nb=len(batch), wdt=wdt,
                           np2=batch[0][0].shape[1]):
                        src = b7[0:np2, 0:nb * 128].rearrange(
                            "p (s i) -> p s i", i=128)[:, :, 0:wdt]
                        return ccopy(eng, mo, src)
                    A(evac_eng, fn, waits=[pe_ev], inc=ve)
                else:
                    for sidx, (in_ap, out_ap, np_) in enumerate(batch):
                        def fn(eng, in_ap=in_ap, out_ap=out_ap, sidx=sidx):
                            src = b7[0:in_ap.shape[1],
                                     sidx * 128: sidx * 128 + in_ap.shape[0]]
                            return ccopy(eng, out_ap, src)
                        A(evac_eng, fn, waits=([pe_ev] if sidx == 0 else ()),
                          inc=(ve if sidx == len(batch) - 1 else None))
                ring_f["evac"] = ve
                last = ve
                i += 4
                bi += 1
            return last

        EV_qkvwT = E("sync", 16)
        A("sync", dma(qkvwT[:].rearrange("p (k c) -> p k c", c=192),
                      qkvw_d.rearrange("(k p) c -> p k c", p=128)),
          inc=EV_qkvwT)

        pstage = [H_s[0:64, 0:2048], H_s[0:64, 2048:4096]]
        rvp = [None, None]
        for i in range(8):
            ld = E("sync", 16)
            w = [rvp[i % 2]] if rvp[i % 2] is not None else []
            A("sync", dma(pstage[i % 2][:],
                          projw_d[:, i * 2048:(i + 1) * 2048]),
              waits=w, inc=ld)
            rv = E("vector")

            def fn(eng, i=i):
                src = pstage[i % 2][:].rearrange("p (o r) -> p o r", r=16)
                dst = projT[0:64, i * 128:(i + 1) * 128].rearrange(
                    "p (o z) -> p o z", z=1)
                return eng.reduce_sum(dst, src, axis=AX.X)
            A("vector", fn, waits=[ld], inc=rv)
            rvp[i % 2] = rv
        EV_projT = rvp[1]

        def wg_piece(i):
            kk, h = i // 4, i % 4
            return (wg_d[kk * 128:(kk + 1) * 128, h * 1024:(h + 1) * 1024],
                    WgT8[:, kk * INNER + h * 1024: kk * INNER + (h + 1) * 1024])

        def gu_pair(w_dram):
            def f(i):
                kk, h = i // 4, i % 4
                return (w_dram[kk * 128:(kk + 1) * 128,
                               h * 1024:(h + 2) * 1024], False)
            return f

        def wd_pair(i):
            src = wd_d[i * 128:(i + 2) * 128, :]
            return src.rearrange("(c p) d -> p c d", p=128), True

        def wu_piece(i):
            kk, h = i // 4, i % 4
            return (wu_d[kk * 128:(kk + 1) * 128, h * 1024:(h + 1) * 1024],
                    WuT8[:, kk * INNER + h * 1024: kk * INNER + (h + 1) * 1024])

        def wd_piece(i):
            return (wd_d[i * 128:(i + 1) * 128, :],
                    WdT8[:, i * D:(i + 1) * D])

        W_SPECS = [(wg_piece, gu_pair(wg_d), wsc_g, wgm127),
                   (wu_piece, gu_pair(wu_d), wsc_u, wum127),
                   (wd_piece, wd_pair, wsc_d, wdm127)]
        slot_reader = [EV_projT, EV_projT, EV_projT, EV_projT]
        gslot = [0]

        def pair_load(pair, i):
            sl = (gslot[0] * 2) % 4
            gslot[0] += 1
            qe = "sync" if gslot[0] % 2 == 0 else "gpsimd"
            ld = Ev(qe, 16, sem="gpdma") if qe == "gpsimd" else E(qe, 16)
            w = [slot_reader[sl], slot_reader[sl + 1]]
            src, is3d = pair(i)
            dst = H_s[:, sl * 1024:(sl + 2) * 1024]
            if is3d:
                dst = dst.rearrange("p (c d) -> p c d", d=1024)
            A(qe, dma(dst, src), waits=w, inc=ld)
            return sl, ld

        def w_pass1():
            vscs = []
            for (piece, pair, wsc, wm127) in W_SPECS:
                rv = None
                for i in range(0, 32, 2):
                    sl, ld = pair_load(pair, i)
                    for j in range(2):
                        rv = E("vector")

                        def fn(eng, sl=sl + j, i=i + j):
                            return eng.reduce_sum(wpart[:, i:i + 1],
                                                  wld4[sl][:], axis=AX.X,
                                                  apply_absolute_value=True)
                        A("vector", fn, waits=[ld, GPC], inc=rv)
                        slot_reader[sl + j] = rv
                rv2 = E("vector")

                def fn(eng):
                    return eng.reduce_sum(t_tot[:], wpart[:], axis=AX.X)
                A("vector", fn, waits=[rv], inc=rv2)
                pe = E("tensor")

                def fn(eng):
                    return eng.matmul(b0[:, 0:1], onesP[:], t_tot[:],
                                      start=True, stop=True)
                A("tensor", fn, waits=[rv2], inc=pe)
                vsc = E("vector")
                vchain([
                    lambda eng: eng.tensor_copy(t_tot[:], b0[:, 0:1]),
                    lambda eng: eng.tensor_single_scalar(
                        t_tot[:], t_tot[:], c_wrec[:], Alu.mult),
                    lambda eng: eng.tensor_single_scalar(
                        t_tot[:], t_tot[:], c_eps[:], Alu.max),
                    lambda eng, wm127=wm127: eng.tensor_single_scalar(
                        wm127[:], t_tot[:], c_1_127[:], Alu.mult),
                    lambda eng, wsc=wsc: eng.reciprocal(wsc[:], t_tot[:]),
                ], first_waits=[pe], last_inc=vsc)
                vscs.append(vsc)
            return vscs

        def w_pass2(vscs):
            prev_v2 = [None, None]
            gp2 = 0
            v2 = None
            v2_wu = None
            for m, (piece, pair, wsc, wm127) in enumerate(W_SPECS):
                for i2 in range(0, 32, 2):
                    sl0, ld = pair_load(pair, i2)
                    info = []
                    for (i, sl) in ((i2, sl0), (i2 + 1, sl0 + 1)):
                        pp = gp2 % 2
                        gp2 += 1
                        src, dst = piece(i)
                        sa = E("scalar")

                        def fn(eng, sl=sl, pp=pp, wsc=wsc):
                            return eng.activation(out=y_pp[pp][:],
                                                  in_=wld4[sl][:],
                                                  func=Act.Identity, bias=0.0,
                                                  scale=wsc[:, 0:1])
                        w = [ld]
                        if i < 2:
                            w.append(vscs[m])
                        if prev_v2[pp] is not None:
                            w.append(prev_v2[pp])
                        A("scalar", fn, waits=w, inc=sa)
                        slot_reader[sl] = sa
                        info.append((pp, dst, sa))
                    les = []
                    for (pp, dst, sa) in info:
                        le = E("vector")

                        def fn(eng, pp=pp):
                            return eng.tensor_single_scalar(
                                t_le_pp[pp][:], y_pp[pp][:], c_nhalf[:],
                                Alu.is_le)
                        A("vector", fn, waits=[sa], inc=le)
                        les.append(le)
                    for kq, (pp, dst, sa) in enumerate(info):
                        v2 = E("vector")

                        def fn(eng, pp=pp, dst=dst):
                            return eng.scalar_tensor_tensor(
                                dst, y_pp[pp][:], c_half[:], t_le_pp[pp][:],
                                Alu.is_ge, Alu.subtract)
                        A("vector", fn, waits=[les[kq]], inc=v2)
                        prev_v2[pp] = v2
                if m == 1:
                    v2_wu = v2
            return v2_wu, v2
        EV_codes = None
        EV_guc = None

        XL = [None] * NG
        EO = [None] * NG
        EV_fin = [None] * (4 * NG)
        EP_state_g = [None] * NG
        prev_xnT_evac = [None]
        prev_vtok_pe = [None]
        prev_down_pe = [None]
        prev_hqT_pe = [None]
        prev_xqT_pe = [None]
        prev_intra3 = [None]
        prev_qx_box = [None]
        prev_qh_box = [None]
        prev_EVH_last2 = [None, None]
        Ag = [dict(xqTe=[None] * 4, qke=[None] * 4, vtok=[None] * 4,
                   ktoke=[None] * 4, guv=[None] * 4) for _ in range(NG)]

        def emit_xload(g):
            XL[g] = E("sync", 16)
            w = ([EO[g - 2]] if g >= 2 else
                 ([EV_codes] if g == 1 else []))
            src = x_d[g * GTOK:(g + 1) * GTOK, :].rearrange(
                "(c p) d -> p c d", p=128)
            dst = x_sb[g % 2][:].rearrange("p (c d) -> p c d", d=D)
            A("sync", dma(dst, src), waits=w, inc=XL[g])

        def emit_A_compute(g, c, gu_extra=None):
            par = g % 2
            k = 4 * g + c
            kp = k % 2
            xg = x_sb[par][:, c * D:(c + 1) * D]

            EV_sum = E("vector")

            def fn(eng, xg=xg, kp=kp):
                return eng.reduce_sum(t_sum[kp][:], xg, axis=AX.X)
            A("vector", fn, waits=[XL[g], GPC], inc=EV_sum)
            EV_negm = E("vector")

            def fn(eng, kp=kp):
                return eng.tensor_single_scalar(
                    t_negm[kp][:], t_sum[kp][:], c_negD[:], Alu.mult)
            A("vector", fn, waits=[EV_sum], inc=EV_negm)

            ES_ssq = E("scalar")

            def fn(eng, xg=xg, kp=kp):
                return eng.activation(out=xn[kp][:], in_=xg,
                                      func=Act.Square,
                                      bias=t_negm[kp][:], scale=1.0,
                                      accum_out=t_ssq[kp][:])
            A("scalar", fn, waits=[EV_negm, prev_xnT_evac[0]], inc=ES_ssq)

            EV_var = E("vector")

            def fn(eng, kp=kp):
                return eng.tensor_single_scalar(t_var[kp][:], t_ssq[kp][:],
                                                c_recD[:], Alu.mult)
            A("vector", fn, waits=[ES_ssq], inc=EV_var)

            ES_sd = E("scalar")

            def fn(eng, kp=kp):
                return eng.activation(out=t_sd[kp][:], in_=t_var[kp][:],
                                      func=Act.Sqrt, bias=c_eps[:], scale=1.0)
            A("scalar", fn, waits=[EV_var], inc=ES_sd)

            EV_rf = E("vector")

            def fn(eng, kp=kp):
                return eng.reciprocal(t_rstd[kp][:], t_sd[kp][:])
            A("vector", fn, waits=[ES_sd], inc=EV_rf)
            EV_nmr = E("vector")

            def fn(eng, kp=kp):
                return eng.tensor_single_scalar(
                    t_nmr[kp][:], t_negm[kp][:], t_rstd[kp][:], Alu.mult)
            A("vector", fn, waits=[EV_rf], inc=EV_nmr)

            ES_xn = E("scalar")

            def fn(eng, xg=xg, kp=kp):
                return eng.activation(out=xn[kp][:], in_=xg,
                                      func=Act.Identity,
                                      bias=t_nmr[kp][:],
                                      scale=t_rstd[kp][:])
            A("scalar", fn, waits=[EV_nmr], inc=ES_xn)

            EV_r127f = E("vector")
            vchain([
                lambda eng, kp=kp: eng.reduce_max(
                    t_amax[kp][:], xn[kp][:], axis=AX.X,
                    apply_absolute_value=True),
                lambda eng, kp=kp: eng.tensor_single_scalar(
                    t_amax[kp][:], t_amax[kp][:], c_eps[:], Alu.max),
                lambda eng, kp=kp: eng.reciprocal(
                    t_rec[kp][:], t_amax[kp][:]),
                lambda eng, kp=kp: eng.tensor_single_scalar(
                    t_r127[kp][:], t_rec[kp][:], c_127[:], Alu.mult),
            ], first_waits=[ES_xn], last_inc=EV_r127f)

            EV_gu = E("vector")

            def fn(eng, k8=k % 8, kp=kp):
                eng.tensor_single_scalar(t_sg[k8][:], t_amax[kp][:],
                                         wgm127[:], Alu.mult)
                return eng.tensor_single_scalar(
                    t_su[k8][:], t_amax[kp][:], wum127[:], Alu.mult)
            A("vector", fn,
              waits=[EV_r127f] + (gu_extra if gu_extra is not None
                                  else [vscs[0], vscs[1]]),
              inc=EV_gu)
            Ag[g]["guv"][c] = EV_gu

            qev = [[E("vector") for _ in range(2)] for _ in range(6)]
            yb = [qs_ye, wei_s]
            for st in range(6):
                for sl in range(2):
                    def fn(eng, st=st, sl=sl, kp=kp):
                        y = yb[sl][:, 0:512]
                        ci = qs_ci8[:, sl * 512:(sl + 1) * 512]
                        f8 = qs_f8[:, sl * 512:(sl + 1) * 512]
                        src = xn[kp][:, sl * 512:(sl + 1) * 512]
                        if st == 0:
                            return eng.tensor_single_scalar(
                                y, src, t_r127[kp][:], Alu.mult)
                        if st == 1:
                            return eng.tensor_copy(ci, y)
                        if st == 2:
                            return eng.scalar_tensor_tensor(
                                f8, ci, c_half[:], y, Alu.add, Alu.is_le)
                        if st == 3:
                            return eng.scalar_tensor_tensor(
                                y, ci, c_nhalf[:], y, Alu.add, Alu.is_ge)
                        if st == 4:
                            return eng.tensor_tensor(f8, f8, y, Alu.subtract)
                        return eng.tensor_tensor(
                            Xq_st[:, sl * 512:(sl + 1) * 512],
                            f8, ci, Alu.add)
                    if st == 0:
                        w = [EV_gu]
                        if sl == 0:
                            w += [prev_xqT_pe[0], prev_intra3[0],
                                  prev_qh_box[0]]
                    else:
                        w = [qev[st - 1][sl]]
                    A("vector", fn, waits=w, inc=qev[st][sl])
            EV_xq = qev[5][1]
            prev_qx_box[0] = EV_xq
            return ES_xn, EV_xq

        def emit_A_pe1(g, c, ES_xn):
            k = 4 * g + c
            kp = k % 2
            A("tensor", None, waits=[ES_xn])
            xnT3 = xnT[kp][:].rearrange("p (s i) -> p s i", i=128)
            ev = f32_transposes(
                [(xn[kp][:, j * 128:(j + 1) * 128],
                  xnT[kp][:, j * 128:(j + 1) * 128], 128)
                 for j in range(8)],
                extra_first_wait=([prev_vtok_pe[0]]
                                  if prev_vtok_pe[0] is not None else []),
                evac_eng="vector",
                merged_out=[xnT3[:, 0:4, :], xnT3[:, 4:8, :]])
            prev_xnT_evac[0] = ev

        def emit_A_pe2(g, c, gu_gate_ev, EV_xq):
            par = g % 2
            k = 4 * g + c
            kp = k % 2
            pe_ev = E("tensor")
            for sidx in range(8):
                def fn(eng, sidx=sidx):
                    return eng.transpose(
                        b6[:, sidx * 128:(sidx + 1) * 128],
                        Xq_st[:, sidx * 128:(sidx + 1) * 128], ident_b[:])
                w = []
                if sidx == 0:
                    w = [EV_xq]
                    if g == 0 and c == 0:
                        w.append(LD_CB)
                    if ring_b["evac"] is not None:
                        w.append(ring_b["evac"])
                A("tensor", fn, waits=w, inc=(pe_ev if sidx == 7 else None))
            prev_xqT_pe[0] = pe_ev
            ve = E("scalar")

            def fn(eng, par=par, c=c):
                dst = XqT_sb[par][:].rearrange(
                    "p (s t) -> p s t", t=GTOK)[:, :, c * 128:(c + 1) * 128]
                src = b6[:].rearrange("p (s i) -> p s i", i=128)
                return ccopy(eng, dst, src)
            w = [pe_ev]
            if gu_gate_ev is not None:
                w.append(gu_gate_ev)
            A("scalar", fn, waits=w, inc=ve)
            ring_b["evac"] = ve
            Ag[g]["xqTe"][c] = ve

            pe_ev = E("tensor")
            for kk in range(8):
                def fn(eng, kk=kk, kp=kp):
                    return eng.matmul(
                        b7[:, 0:128],
                        qkvwT[:, kk * 192: kk * 192 + 128],
                        xnT[kp][:, kk * 128:(kk + 1) * 128],
                        start=(kk == 0), stop=(kk == 7))
                w = []
                if kk == 0:
                    w = [ring_f["evac"], EV_qkvwT]
                A("tensor", fn, waits=w, inc=(pe_ev if kk == 7 else None))
            ve = E("scalar")

            def fn(eng, par=par, c=c):
                eng.activation(out=q_sb[par][0:64, c * 128:(c + 1) * 128],
                               in_=b7[0:64, 0:128], func=Act.Identity,
                               bias=qb_qk[0:64, 0:1], scale=1.0)
                return eng.activation(
                    out=kT_sb[par][0:64, c * 128:(c + 1) * 128],
                    in_=b7[64:128, 0:128], func=Act.Identity,
                    bias=qb_qk[64:128, 0:1], scale=1.0)
            A("scalar", fn, waits=[pe_ev, LD_qb], inc=ve)
            ring_f["evac"] = ve
            Ag[g]["qke"][c] = ve

            pe_v = E("tensor")
            for kk in range(9):
                def fn(eng, kk=kk, kp=kp):
                    dst = b7[:, 0:64]
                    if kk < 8:
                        return eng.matmul(
                            dst, xnT[kp][:, kk * 128:(kk + 1) * 128],
                            qkvwT[:, kk * 192 + 128: kk * 192 + 192],
                            start=(kk == 0), stop=False)
                    return eng.matmul(dst, ones_row, qb_vrow[:],
                                      start=False, stop=True)
                w = []
                if kk == 0:
                    w = [ring_f["evac"], LD_qbv]
                A("tensor", fn, waits=w, inc=(pe_v if kk == 8 else None))
            pe_k = E("tensor")
            for kk in range(9):
                def fn(eng, kk=kk, kp=kp):
                    dst = b7[:, 64:128]
                    if kk < 8:
                        return eng.matmul(
                            dst, xnT[kp][:, kk * 128:(kk + 1) * 128],
                            qkvwT[:, kk * 192 + 64: kk * 192 + 128],
                            start=(kk == 0), stop=False)
                    return eng.matmul(dst, ones_row, qb_krow[:],
                                      start=False, stop=True)
                w = [LD_qbk] if kk == 0 else []
                A("tensor", fn, waits=w, inc=(pe_k if kk == 8 else None))
            prev_vtok_pe[0] = pe_k
            ve = E("scalar")

            def fn(eng, par=par, c=c):
                ccopy(eng, v_sb[par][:, c * HS:(c + 1) * HS], b7[:, 0:64])
                return ccopy(eng, k_sb[par][:, c * HS:(c + 1) * HS],
                             b7[:, 64:128])
            A("scalar", fn, waits=[pe_v, pe_k], inc=ve)
            ring_f["evac"] = ve
            Ag[g]["vtok"][c] = ve
            Ag[g]["ktoke"][c] = ve

        def emit_B_front(g, prev_silu7):
            par = g % 2
            EP_inter = E("tensor")

            def fn(eng, par=par):
                return eng.matmul(b4[0:64, 0:GTOK], S_sb[:], q_sb[par][:],
                                  start=True, stop=False)
            w = [Ag[g]["qke"][3]]
            if EV_S_prev_box[0] is not None:
                w.append(EV_S_prev_box[0])
            if prev_silu7 is not None:
                w.append(prev_silu7)
            A("tensor", fn, waits=w, inc=EP_inter)

            EV_wei = [None] * 4
            EP_intra = [None] * 4
            for c in range(4):
                bank = b2 if c % 2 == 0 else b3
                ncols = GTOK - c * 128
                pe_ev = E("tensor")

                def fn(eng, c=c, bank=bank, ncols=ncols, par=par):
                    return eng.matmul(bank[:, 0:ncols],
                                      kT_sb[par][0:64, c * 128:(c + 1) * 128],
                                      q_sb[par][0:64, c * 128:GTOK],
                                      start=True, stop=True)
                w = [Ag[g]["qke"][c]]
                if c >= 2:
                    w.append(EV_wei[c - 2])
                elif g > 0:
                    w.append(prev_EVH_last2[c])
                A("tensor", fn, waits=w, inc=pe_ev)
                ve = E("vector")

                def fn(eng, c=c, bank=bank, ncols=ncols):
                    wsl = wei_sb[c % 2]
                    last = eng.tensor_tensor(wsl[:, 0:128], bank[:, 0:128],
                                             mask_keep[:], Alu.mult)
                    if ncols > 128:
                        last = eng.tensor_copy(wsl[:, 128:ncols],
                                               bank[:, 128:ncols])
                    return last
                w = [pe_ev]
                if c >= 1:
                    w.append(EP_intra[c - 1])
                A("vector", fn, waits=w, inc=ve)
                EV_wei[c] = ve
                pe_ev = E("tensor")

                def fn(eng, c=c, ncols=ncols, par=par):
                    return eng.matmul(b4[0:64, c * 128:GTOK],
                                      v_sb[par][:, c * HS:(c + 1) * HS],
                                      wei_sb[c % 2][:, 0:ncols],
                                      start=False, stop=(c == 3))
                A("tensor", fn, waits=[ve, Ag[g]["vtok"][c]], inc=pe_ev)
                EP_intra[c] = pe_ev
            prev_intra3[0] = EP_intra[3]

            EP_state = E("tensor")
            for c in range(4):
                def fn(eng, c=c, par=par):
                    return eng.matmul(b5[0:64, 0:64],
                                      k_sb[par][:, c * HS:(c + 1) * HS],
                                      v_sb[par][:, c * HS:(c + 1) * HS],
                                      start=(c == 0), stop=(c == 3))
                w = []
                if c == 0:
                    w = [Ag[g]["ktoke"][3], EP_inter]
                    if g > 0:
                        w.append(prev_EVH_last2[1])
                A("tensor", fn, waits=w, inc=(EP_state if c == 3 else None))
            EP_state_g[g] = EP_state

            EV_S_upd = E("vector")

            def fn(eng):
                return eng.tensor_tensor(S_sb[:], S_sb[:], b5[0:64, 0:64],
                                         Alu.add)
            A("vector", fn, waits=[EP_state], inc=EV_S_upd)
            EV_S_prev_box[0] = EV_S_upd

            EV_outh = E("scalar")

            def fn(eng, par=par):
                return ccopy(eng, outhT[par][0:64, :], b4[0:64, :])
            A("scalar", fn, waits=[EP_intra[3]], inc=EV_outh)
            return EV_wei, EV_outh, EV_S_upd

        def emit_B_proj(g, EV_outh, extra0=()):
            par = g % 2
            EV_xp = [None] * 4
            for c in range(4):
                pe_ev = E("tensor")

                def fn(eng, c=c, par=par):
                    eng.matmul(b0[:], outhT[par][:, c * 128:(c + 1) * 128],
                               projT[:, 0:512], start=True, stop=True)
                    return eng.matmul(b1[:],
                                      outhT[par][:, c * 128:(c + 1) * 128],
                                      projT[:, 512:1024],
                                      start=True, stop=True)
                w = [EV_outh, EV_projT, LD_pb]
                if c == 0:
                    w += list(extra0)
                    if g > 0:
                        w.append(EV_fin[4 * g - 1])
                else:
                    w.append(EV_xp[c - 1])
                A("tensor", fn, waits=w, inc=pe_ev)
                ve = E("vector")

                def fn(eng, c=c, par=par):
                    xs = x_sb[par]
                    eng.tensor_tensor(xs[:, c * D: c * D + 512],
                                      xs[:, c * D: c * D + 512], b0[:],
                                      Alu.add)
                    return eng.tensor_tensor(xs[:, c * D + 512:(c + 1) * D],
                                             xs[:, c * D + 512:(c + 1) * D],
                                             b1[:], Alu.add)
                A("vector", fn, waits=[pe_ev], inc=ve)
                EV_xp[c] = ve
            return EV_xp

        EV_S_prev_box = [None]


        emit_xload(0)
        vscs = w_pass1()
        for c in range(4):
            _esxn, _evxq = emit_A_compute(0, c, gu_extra=[vscs[0], vscs[1]])
            emit_A_pe1(0, c, _esxn)
            emit_A_pe2(0, c, None, _evxq)
        prev_silu7_box = [None]
        _w, _o, _su = emit_B_front(0, None)
        _xp = emit_B_proj(0, _o, extra0=[vscs[2]])
        Bg = {0: (_w, _o, _su, _xp)}
        EV_guc, EV_codes = w_pass2(vscs)

        for g in range(NG):
            par = g % 2
            EV_wei, EV_outh, EV_S_upd, EV_xp = Bg[g]
            if g + 1 < NG:
                emit_xload(g + 1)

            def flush_hqT(pk, pc, hq_evs):
                kpp = pk % 2
                hqTe_last = None
                for bb in range(4):
                    pe_ev = E("tensor")
                    for sidx in range(8):
                        j = bb * 8 + sidx

                        def fn(eng, j=j, sidx=sidx, kpp=kpp):
                            return eng.transpose(
                                b6[:, sidx * 128:(sidx + 1) * 128],
                                Hq[kpp][:, j * 128:(j + 1) * 128],
                                ident_b[:])
                        w = []
                        if sidx == 0:
                            w = [hq_evs[2 * bb + 1]]
                            if ring_b["evac"] is not None:
                                w.append(ring_b["evac"])
                        A("tensor", fn, waits=w,
                          inc=(pe_ev if sidx == 7 else None))
                    if bb == 3:
                        prev_hqT_pe[0] = pe_ev
                    ve = E("scalar")

                    def fn(eng, bb=bb):
                        return ccopy(eng,
                                     HqT_s[:, bb * 1024:(bb + 1) * 1024],
                                     b6[:])
                    w = [pe_ev]
                    if bb == 0 and prev_down_pe[0] is not None:
                        w.append(prev_down_pe[0])
                    A("scalar", fn, waits=w, inc=ve)
                    ring_b["evac"] = ve
                    hqTe_last = ve
                return hqTe_last

            def flush_down(pk, pc, hqTe_last, dqh_ev):
                EP_down = E("tensor")
                for j in range(32):
                    for n2 in range(2):
                        bank = b0 if n2 == 0 else b1

                        def fn(eng, j=j, bank=bank, n2=n2):
                            return eng.matmul(
                                bank[:],
                                HqT_s[:, j * 128:(j + 1) * 128],
                                WdT8[:, j * D + n2 * 512:
                                     j * D + (n2 + 1) * 512],
                                start=(j == 0), stop=(j == 31))
                        w = []
                        if j == 0 and n2 == 0:
                            w = [hqTe_last, EV_xp[3]]
                            if pk == 0:
                                w.append(EV_codes)
                            if pk > 0 and EV_fin[pk - 1] is not None:
                                w.append(EV_fin[pk - 1])
                        A("tensor", fn, waits=w,
                          inc=(EP_down if (j == 31 and n2 == 1) else None))
                prev_down_pe[0] = EP_down
                fin = E("vector")

                def fn(eng, pc=pc, par=par, pk=pk):
                    xs = x_sb[par]
                    kpp = pk % 2
                    eng.scalar_tensor_tensor(
                        xs[:, pc * D: pc * D + 512], b0[:],
                        t_dqh[kpp][:], xs[:, pc * D: pc * D + 512],
                        Alu.mult, Alu.add)
                    return eng.scalar_tensor_tensor(
                        xs[:, pc * D + 512:(pc + 1) * D], b1[:],
                        t_dqh[kpp][:],
                        xs[:, pc * D + 512:(pc + 1) * D],
                        Alu.mult, Alu.add)
                A("vector", fn, waits=[EP_down, dqh_ev], inc=fin)
                EV_fin[pk] = fin

            pend = None
            hqTe_pend = None
            for c in range(4):
                k = 4 * g + c
                kp = k % 2
                if g + 1 < NG:
                    a_es_xn, a_ev_xq = emit_A_compute(g + 1, c)
                EV_H = [None] * 8
                gu_chunk_last = None
                for n in range(8):
                    Gb = b2 if n % 2 == 0 else b4
                    Ub = b3 if n % 2 == 0 else b5
                    pe_ev = E("tensor")
                    for kk in range(8):
                        def fn(eng, kk=kk, n=n, Gb=Gb, Ub=Ub, par=par, c=c):
                            lhs = XqT_sb[par][:, kk * GTOK + c * 128:
                                              kk * GTOK + (c + 1) * 128]
                            eng.matmul(Gb[:], lhs,
                                       WgT8[:, kk * INNER + n * 512:
                                            kk * INNER + (n + 1) * 512],
                                       start=(kk == 0), stop=(kk == 7))
                            return eng.matmul(
                                Ub[:], lhs,
                                WuT8[:, kk * INNER + n * 512:
                                     kk * INNER + (n + 1) * 512],
                                start=(kk == 0), stop=(kk == 7))
                        w = []
                        if kk == 0:
                            w = [Ag[g]["xqTe"][c], EV_guc]
                            if n >= 2:
                                w.append(EV_H[n - 2])
                            elif c == 0:
                                if n == 0:
                                    w += [EV_wei[3], EV_wei[2]]
                                else:
                                    w += [EV_outh, EV_S_upd]
                            else:
                                w.append(prev_EVH_last2[n])
                        A("tensor", fn, waits=w,
                          inc=(pe_ev if kk == 7 else None))
                    gu_chunk_last = pe_ev
                    if n == 1 and pend is not None:
                        hqTe_pend = flush_hqT(pend[0], pend[1], pend[2])
                    if n == 2 and g + 1 < NG:
                        emit_A_pe1(g + 1, c, a_es_xn)

                    sa = E("scalar")

                    def fn(eng, Gb=Gb, k8=k % 8):
                        return eng.activation(out=g_act_s[:], in_=Gb[:],
                                              func=Act.Silu, bias=0.0,
                                              scale=t_sg[k8][:, 0:1])
                    w = [pe_ev, Ag[g]["guv"][c]]
                    if n >= 1:
                        w.append(EV_H[n - 1])
                    elif c > 0:
                        w.append(prev_EVH_last2[1])
                    A("scalar", fn, waits=w, inc=sa)
                    if n == 7:
                        prev_silu7_box[0] = sa
                    ve = E("vector")

                    def fn(eng, n=n, Ub=Ub, kp=kp, k8=k % 8):
                        return eng.scalar_tensor_tensor(
                            H[kp][:, n * 512:(n + 1) * 512], Ub[:],
                            t_su[k8][:, 0:1], g_act_s[:],
                            Alu.mult, Alu.mult)
                    A("vector", fn, waits=[sa], inc=ve)
                    EV_H[n] = ve
                    if n >= 1:
                        vp = E("vector")

                        def fn(eng, n=n, kp=kp):
                            return eng.reduce_max(
                                amx8[:, n - 1:n],
                                H[kp][:, (n - 1) * 512:n * 512],
                                axis=AX.X, apply_absolute_value=True)
                        A("vector", fn, waits=[EV_H[n - 1]], inc=vp)
                prev_EVH_last2 = [EV_H[6], EV_H[7]]
                EV_amx_last = E("vector")

                def fn(eng, kp=kp):
                    return eng.reduce_max(amx8[:, 7:8],
                                          H[kp][:, 7 * 512:8 * 512],
                                          axis=AX.X,
                                          apply_absolute_value=True)
                A("vector", fn, waits=[EV_H[7]], inc=EV_amx_last)

                ev_ha = E("vector")

                def fn(eng, kp=kp):
                    return eng.reduce_max(t_amxh[kp][:], amx8[:], axis=AX.X)
                A("vector", fn, waits=[EV_amx_last], inc=ev_ha)
                ev_hb = E("vector")

                def fn(eng, kp=kp):
                    return eng.tensor_single_scalar(
                        t_amxh[kp][:], t_amxh[kp][:], c_eps[:], Alu.max)
                A("vector", fn, waits=[ev_ha], inc=ev_hb)
                ev_hc = E("vector")

                def fn(eng, kp=kp):
                    return eng.reciprocal(t_rech[kp][:], t_amxh[kp][:])
                A("vector", fn, waits=[ev_hb], inc=ev_hc)
                EV_hf = E("vector")

                def fn(eng, kp=kp):
                    eng.tensor_single_scalar(t_dqh[kp][:], t_amxh[kp][:],
                                             wdm127[:], Alu.mult)
                    return eng.tensor_single_scalar(
                        t_r127h[kp][:], t_rech[kp][:], c_127[:], Alu.mult)
                A("vector", fn, waits=[ev_hc], inc=EV_hf)
                EV_dqh = EV_hf

                EV_hqs = [E("vector") for _ in range(8)]
                hprev = None
                for t2 in range(4):
                    hev = [[E("vector") for _ in range(2)] for _ in range(6)]
                    for st in range(6):
                        for j in range(2):
                            def fn(eng, t2=t2, st=st, j=j, kp=kp):
                                sl = 2 * t2 + j
                                hs = H[kp][:, sl * 512:(sl + 1) * 512]
                                ci = qs_ci8[:, j * 512:(j + 1) * 512]
                                f8 = qs_f8[:, j * 512:(j + 1) * 512]
                                if st == 0:
                                    return eng.tensor_single_scalar(
                                        hs, hs, t_r127h[kp][:], Alu.mult)
                                if st == 1:
                                    return eng.tensor_copy(ci, hs)
                                if st == 2:
                                    return eng.scalar_tensor_tensor(
                                        f8, ci, c_half[:], hs,
                                        Alu.add, Alu.is_le)
                                if st == 3:
                                    return eng.scalar_tensor_tensor(
                                        hs, ci, c_nhalf[:], hs,
                                        Alu.add, Alu.is_ge)
                                if st == 4:
                                    return eng.tensor_tensor(f8, f8, hs,
                                                             Alu.subtract)
                                sl = 2 * t2 + j
                                return eng.tensor_tensor(
                                    Hq[kp][:, sl * 512:(sl + 1) * 512],
                                    f8, ci, Alu.add)
                            if st == 0:
                                w = [EV_dqh]
                                if t2 == 0 and j == 0:
                                    if prev_hqT_pe[0] is not None:
                                        w.append(prev_hqT_pe[0])
                                    if prev_qx_box[0] is not None:
                                        w.append(prev_qx_box[0])
                                if hprev is not None:
                                    w.append(hprev)
                            else:
                                w = [hev[st - 1][j]]
                            A("vector", fn, waits=w, inc=hev[st][j])
                    hprev = hev[5][1]
                    EV_hqs[2 * t2 + 1] = hev[5][1]
                prev_qh_box[0] = hprev

                if g + 1 < NG:
                    emit_A_pe2(g + 1, c, gu_chunk_last, a_ev_xq)

                if pend is not None:
                    flush_down(pend[0], pend[1], hqTe_pend, pend[3])
                pend = (k, c, EV_hqs, EV_dqh)

            if g + 1 < NG:
                _w, _o, _su = emit_B_front(g + 1, prev_silu7_box[0])
            hqTe_pend = flush_hqT(pend[0], pend[1], pend[2])
            flush_down(pend[0], pend[1], hqTe_pend, pend[3])
            if g + 1 < NG:
                _xp = emit_B_proj(g + 1, _o)
                Bg[g + 1] = (_w, _o, _su, _xp)

            EO[g] = E("sync", 16)
            dstap = y_d[g * GTOK:(g + 1) * GTOK, :].rearrange(
                "(c p) d -> p c d", p=128)
            osrc = x_sb[par][:].rearrange("p (c d) -> p c d", d=D)
            A("sync", dma(dstap, osrc), waits=[EV_fin[4 * g + 3]],
              inc=EO[g])

        A("sync", None, waits=[EO[NG - 1]])

        p.assign()
        p.check()

        def emit(eng_obj, eng_name):
            cum = {}
            for fn, waits, inc in p.ops[eng_name]:
                if inc is not None and inc.amount == 16:
                    cb = cum.get(inc.sem, 0)
                    if cb > 0:
                        eng_obj.wait_ge(sems[inc.sem], cb)
                if inc is not None:
                    cum[inc.sem] = cum.get(inc.sem, 0) + inc.amount
                for w in waits:
                    eng_obj.wait_ge(sems[w.sem], w.val)
                instr = fn(eng_obj) if fn is not None else None
                if inc is not None:
                    instr.then_inc(sems[inc.sem], inc.amount)

        @block.sync
        def _(e):
            emit(e, "sync")

        @block.gpsimd
        def _(e):
            emit(e, "gpsimd")

        @block.vector
        def _(e):
            emit(e, "vector")

        @block.scalar
        def _(e):
            emit(e, "scalar")

        @block.tensor
        def _(e):
            emit(e, "tensor")

    return nc


_NC_CACHE = {}


def common_inputs(qkv_w, qkv_b, proj_w, proj_b, w_gate, w_up, w_down):
    pw = np.asarray(proj_w, np.float32)
    return {
        "qkv_w": np.ascontiguousarray(np.asarray(qkv_w, np.float32).T),
        "qkv_b": np.ascontiguousarray(np.asarray(qkv_b, np.float32)),
        "proj_w": np.ascontiguousarray(
            pw.reshape(1024, 16, 64).transpose(2, 0, 1).reshape(64, 16384)),
        "proj_b": np.ascontiguousarray(np.asarray(proj_b, np.float32)),
        "w_gate": np.ascontiguousarray(np.asarray(w_gate, np.float32).T),
        "w_up": np.ascontiguousarray(np.asarray(w_up, np.float32).T),
        "w_down": np.ascontiguousarray(np.asarray(w_down, np.float32).T),
        "consts_f": np.eye(128, dtype=np.float32),
        "consts_b": np.ascontiguousarray(np.concatenate(
            [np.eye(128), np.triu(np.ones((128, 128)))],
            axis=1).astype(ml_dtypes.bfloat16)),
    }


def kernel(L0_attn_ids, L0_attn_qkv_w, L0_attn_qkv_b, L0_attn_proj_w,
           L0_attn_proj_b, L0_ffn_w_gate, L0_ffn_w_up, L0_ffn_w_down):
    x = np.ascontiguousarray(np.asarray(L0_attn_ids, dtype=np.float32))
    B = x.shape[0]
    if "nc" not in _NC_CACHE:
        _NC_CACHE["nc"] = build_program()
    nc = _NC_CACHE["nc"]
    common = common_inputs(L0_attn_qkv_w, L0_attn_qkv_b, L0_attn_proj_w,
                           L0_attn_proj_b, L0_ffn_w_gate, L0_ffn_w_up,
                           L0_ffn_w_down)
    in_maps = [dict(common, x=np.ascontiguousarray(x[b])) for b in range(B)]
    res = run_bass_kernel_spmd(nc, in_maps, core_ids=list(range(B)))
    return np.stack([np.asarray(r["y"], dtype=np.float32)
                     for r in res.results], axis=0)
